# version 05
# speedup vs baseline: 1.8089x; 1.0021x over previous
import numpy as np
from contextlib import ExitStack

import concourse.bass as bass
import concourse.mybir as mybir
from concourse.alu_op_type import AluOpType as OP
from concourse.bass_utils import run_bass_kernel_spmd

F32 = mybir.dt.float32
BF16 = mybir.dt.bfloat16
AF = mybir.ActivationFunctionType

NCORES = 8
T = 8192
TC = 1024
TCE = 1152
NCH = 8
D = 2048
KD = 16
H0, N0, P0 = 64, 128, 64
DIN0 = 4096
EPS0 = 1e-6
KH, VH, DK, DV = 16, 32, 128, 128
VDIM = 4096
CONVD = 8192
INTER = 5632



class Sched:
    NQ = 4
    SEMS = (tuple(f"dS{i}" for i in range(4)) + tuple(f"dG{i}" for i in range(4))
            + tuple(f"dO{i}" for i in range(4)) + ("cT", "cV", "cS", "cP"))
    STORE_SEMS = tuple(f"dO{i}" for i in range(4))
    OWN = {"tensor": "cT", "vector": "cV", "scalar": "cS", "gpsimd": "cP"}

    def __init__(self):
        self.ops = []
        self.cnt = {k: 0 for k in Sched.SEMS}
        self.ndma = {"dS": 0, "dG": 0, "dO": 0}

    def op(self, engine, fn, sem, inc, dma_pre=None):
        self.ops.append((engine, fn, sem, inc, dict(self.cnt), dma_pre))
        if sem:
            self.cnt[sem] += inc

    def t(self, fn):
        self.op("tensor", fn, "cT", 1)

    def v(self, fn):
        self.op("vector", fn, "cV", 1)

    def s(self, fn):
        self.op("scalar", fn, "cS", 1)

    def dma(self, out, in_, cast=False):
        is_store = type(getattr(out, "tensor", None)).__name__ == "DRamTensorHandle"
        if cast:
            q, eng = "dG", "gpsimd"
        elif is_store:
            q, eng = "dO", "sync"
        else:
            q, eng = "dS", "sync"
        k = self.ndma[q]
        self.ndma[q] += 1
        sem = f"{q}{k % Sched.NQ}"
        ord_ = k // Sched.NQ
        self.op(eng, lambda e, o=out, i=in_: e.dma_start(out=o, in_=i), sem, 16,
                dma_pre=(sem, 16 * ord_))

    MODE = "full"
    AUTO_WAITS = None

    LIMIT = None
    FINAL_CAP = None

    def build(self, blk, sems):
        Sched.LAST = self
        lim = Sched.LIMIT if Sched.LIMIT is not None else len(self.ops)
        by_eng = {}
        fcnt = {k: 0 for k in Sched.SEMS}
        for gidx, (engine, fn, sem, inc, need, dma_pre) in enumerate(self.ops):
            if gidx >= lim:
                continue
            by_eng.setdefault(engine, []).append((fn, sem, inc, need, dma_pre))
            if sem:
                fcnt[sem] += inc
        final = fcnt
        if Sched.FINAL_CAP:
            for k, v in Sched.FINAL_CAP.items():
                final[k] = min(final[k], v)
        mode = Sched.MODE

        def mk(name):
            def body(eng):
                seen = {k: 0 for k in final}
                own = Sched.OWN.get(name)
                relax = {"autope": ("tensor",),
                         "autopv": ("tensor", "vector"),
                         "autopvp": ("tensor", "vector", "gpsimd"),
                         "autopvps": ("tensor", "vector", "gpsimd", "scalar"),
                         "autopvpsc": ("tensor", "vector", "gpsimd", "scalar"),
                         "autopvpd": ("tensor", "vector", "gpsimd", "sync"),
                         "autopvpdx": ("tensor", "vector", "gpsimd", "sync"),
                         "auto": ("tensor", "vector", "gpsimd", "scalar", "sync")}
                for idx, (fn, sem, inc, need, dma_pre) in enumerate(by_eng.get(name, [])):
                    if mode == "full":
                        req = dict(need)
                    elif mode == "none":
                        req = {}
                    elif name in relax.get(mode, ()):
                        req = dict(Sched.AUTO_WAITS.get((name, idx), {}))
                    elif mode == "autopvpdx" and name == "scalar":
                        aw = Sched.AUTO_WAITS.get((name, idx), {})
                        req = {k: v for k, v in need.items() if k[0] == "c"}
                        for k, v in aw.items():
                            if k[0] == "d" and v > req.get(k, 0):
                                req[k] = v
                    else:
                        req = dict(need)
                    if dma_pre is not None and mode != "none":
                        ps, pv = dma_pre
                        if req.get(ps, 0) < pv:
                            req[ps] = pv
                    for s2 in Sched.SEMS:
                        v = req.get(s2, 0)
                        if v > seen[s2]:
                            eng.wait_ge(sems[s2], v)
                            seen[s2] = v
                    inst = fn(eng)
                    if sem:
                        inst.then_inc(sems[sem], inc)
                for s2 in Sched.SEMS:
                    v = final[s2]
                    if v > seen[s2]:
                        eng.wait_ge(sems[s2], v)
                        seen[s2] = v
            return body

        for name in ("sync", "gpsimd", "tensor", "vector", "scalar"):
            getattr(blk, name)(mk(name))


ENG2BIR = {"sync": "SP", "gpsimd": "Pool", "tensor": "PE", "vector": "DVE",
           "scalar": "Activation"}
BIR2ENG = {v: k for k, v in ENG2BIR.items()}
_REAL_OPS = ("InstDMACopy", "InstMemset", "InstTensorTensor", "InstTensorScalarPtr",
             "InstMatmult", "InstActivate", "InstTensorReduce", "InstTensorCopy",
             "InstReciprocal", "InstTensorSelect", "InstCopy", "InstActivation",
             "InstPool", "InstIota", "InstShift")


def _pap_interval(a):
    off = getattr(a, "offset", None)
    ap = getattr(a, "ap", None)
    mr = getattr(a, "memref", None)
    if off is None or ap is None or mr is None:
        return None
    span = 1
    try:
        for st, cnt in ap:
            span += abs(int(st)) * (int(cnt) - 1)
    except TypeError:
        ap2 = [list(p) for p in ap]
        for st, cnt in ap2:
            span += abs(int(st)) * (int(cnt) - 1)
    return (str(mr), int(off), int(off) + span)


def harvest_deps(nc, sch):
    per_eng_insts = {}
    for inst in nc.all_instructions():
        tn = type(inst).__name__
        en = getattr(inst, "engine", None)
        en = getattr(en, "name", str(en))
        if en not in BIR2ENG:
            continue
        if tn not in _REAL_OPS:
            continue
        outs = [x for x in (_pap_interval(a) for a in getattr(inst, "outs", [])) if x]
        ins = [x for x in (_pap_interval(a) for a in getattr(inst, "ins", [])) if x]
        if not outs and not ins:
            continue
        if outs and all(o[0].startswith("const-") for o in outs) and not ins:
            continue
        per_eng_insts.setdefault(BIR2ENG[en], []).append((ins, outs))
    eng_idx = {k: 0 for k in ENG2BIR}
    cum = {k: 0 for k in Sched.SEMS}
    recs = []
    for (engine, fn, sem, inc, need, dma_pre) in sch.ops:
        i = eng_idx[engine]
        eng_idx[engine] += 1
        lst = per_eng_insts.get(engine, [])
        if i < len(lst):
            ins, outs = lst[i]
        else:
            ins, outs = [], []
        pre = dict(cum)
        if sem:
            cum[sem] += inc
        recs.append((engine, i, ins, outs, sem, dict(cum), pre))
    for k in ENG2BIR:
        got = len(per_eng_insts.get(k, []))
        want = eng_idx[k]
        if got != want:
            raise RuntimeError(f"harvest mismatch {k}: {got} insts vs {want} ops")
    writers = {}
    readers = {}
    waits = {}
    for opid, (engine, i, ins, outs, sem, _c, _p) in enumerate(recs):
        dep = set()
        for (mr, lo, hi) in ins:
            for (l2, h2, _e), j in writers.get(mr, {}).items():
                if l2 < hi and lo < h2:
                    dep.add(j)
        for (mr, lo, hi) in outs:
            for (l2, h2, _e), j in writers.get(mr, {}).items():
                if l2 < hi and lo < h2:
                    dep.add(j)
            for (l2, h2, _e), j in readers.get(mr, {}).items():
                if l2 < hi and lo < h2:
                    dep.add(j)
        req = {}
        for j in dep:
            je, ji, jins, jouts, jsem, jcum, jpre = recs[j]
            if jsem is None:
                continue
            if je == "tensor" and engine == "tensor":
                continue
            v = jcum[jsem]
            if req.get(jsem, 0) < v:
                req[jsem] = v
        if sem in (None, "cT", "cV", "cS", "cP"):
            mypre = recs[opid][6]
            qs = Sched.STORE_SEMS if engine != "scalar" else [
                q for q in Sched.SEMS if q[0] == "d"]
            for q in qs:
                v = mypre[q]
                if v > req.get(q, 0):
                    req[q] = v
            if engine == "scalar" and i > 0 and req.get("cS", 0) < i:
                req["cS"] = i
        if req:
            waits[(engine, i)] = req
        for (mr, lo, hi) in outs:
            wl = writers.setdefault(mr, {})
            for key in [k for k, _ in wl.items() if lo <= k[0] and k[1] <= hi]:
                del wl[key]
            wl[(lo, hi, engine)] = opid
            rl = readers.setdefault(mr, {})
            for key in [k for k, _ in rl.items() if lo <= k[0] and k[1] <= hi]:
                del rl[key]
        for (mr, lo, hi) in ins:
            readers.setdefault(mr, {})[(lo, hi, engine)] = opid
    return waits


AUTO_MODE = "autopvpdx"


def build_auto(build_fn):
    Sched.MODE = "none"
    nc1 = build_fn()
    waits = harvest_deps(nc1, Sched.LAST)
    Sched.MODE = AUTO_MODE
    Sched.AUTO_WAITS = waits
    nc2 = build_fn()
    Sched.MODE = "full"
    Sched.AUTO_WAITS = None
    return nc2


def _perm_n():
    return np.concatenate([np.arange(0, N0, 2), np.arange(1, N0, 2)])


def host_consts():
    c = {}
    c["ones_col"] = np.ones((128, 1), np.float32)
    c["ones_row"] = np.ones((1, 128), np.float32)
    c["ident"] = np.eye(128, dtype=np.float32)
    ju = np.triu(np.ones((128, 128), np.float32), 1)
    c["mask_strict_add"] = (ju - 1.0) * 1e30
    jl = np.triu(np.ones((128, 128), np.float32), 0)
    c["mask_incl_add"] = (jl - 1.0) * 1e30
    lo = np.tril(np.ones((128, 128), np.float32), -1)
    c["mask_lowstrict_add"] = (lo - 1.0) * 1e30
    c["tri_incl"] = np.triu(np.ones((128, 128), np.float32), 0)
    c["exp8"] = np.repeat(np.eye(8, dtype=np.float32), 16, axis=1)
    bd = np.zeros((128, 128), np.float32)
    for b0 in range(0, 128, 32):
        bd[b0:b0 + 32, b0:b0 + 32] = 1
    c["bd32"] = bd
    bl = np.tril(np.ones((128, 128), np.float32), -1) * (1 - bd)
    c["bl32"] = bl
    c["bl32T"] = np.ascontiguousarray(bl.T)
    return c


def prep_og(Wt, w=512):
    Dd, O = Wt.shape
    return np.ascontiguousarray(Wt.reshape(Dd, O // w, w).transpose(1, 0, 2)).astype(np.float32)



def build_A():
    nc = bass.Bass("TRN2", target_bir_lowering=False, name="mkA")

    def DR(n, sh, dt, k=None):
        return nc.dram_tensor(n, list(sh), dt, kind=k).ap() if k else nc.dram_tensor(n, list(sh), dt).ap()

    x_in = DR("x_ext", (TCE, D), F32, "ExternalInput")
    WBt = DR("WBt", (16, D, 512), F32, "ExternalInput")
    WCt = DR("WCt", (16, D, 512), F32, "ExternalInput")
    Wxt = DR("Wxt", (8, D, 512), F32, "ExternalInput")
    Wzt = DR("Wzt", (8, D, 512), F32, "ExternalInput")
    Wtht = DR("Wtht", (8, D, 512), F32, "ExternalInput")
    Wdlt = DR("Wdlt", (D, 128), F32, "ExternalInput")
    Wfft = DR("Wfft", (4, D, 512), F32, "ExternalInput")
    Aall = DR("Aall", (D, 128), F32, "ExternalInput")
    Bstk = DR("Bstk", (128, 2048), F32, "ExternalInput")
    gwT = DR("gwT", (D, 8), F32, "ExternalInput")
    gb = DR("gb", (8, 1), F32, "ExternalInput")
    ffb = DR("ffb", (1, 2048), F32, "ExternalInput")
    A_log = DR("A_log", (1, H0), F32, "ExternalInput")
    nw_in = DR("nw_in", (N0, 2), F32, "ExternalInput")
    bias_in = DR("bias_in", (N0, 2 * H0), F32, "ExternalInput")
    HC = host_consts()
    CONST = {k: DR("c_" + k, v.shape, F32, "ExternalInput") for k, v in HC.items()}

    o_Bn = DR("o_Bn", (64, 128, TC), BF16, "ExternalOutput")
    o_Cn = DR("o_Cn", (64, 128, TC), BF16, "ExternalOutput")
    o_xh = DR("o_xh", (TC, 4096), BF16, "ExternalOutput")
    o_z = DR("o_z", (TC, 4096), BF16, "ExternalOutput")
    o_thc = DR("o_thc", (4096, TC), F32, "ExternalOutput")
    o_tht = DR("o_tht", (1, 4096), F32, "ExternalOutput")
    o_lam_tm = DR("o_lam_tm", (TC, H0), F32, "ExternalOutput")
    o_lam_fm = DR("o_lam_fm", (H0, TC), F32, "ExternalOutput")
    o_cg_fm = DR("o_cg_fm", (128, TC), F32, "ExternalOutput")
    o_accx = DR("o_accx", (TC, D), F32, "ExternalOutput")
    o_xnf = DR("o_xnf", (16, 128, TC), BF16, "ExternalOutput")

    sch = Sched()
    with ExitStack() as stk:
        def S(n, sh, dt):
            return stk.enter_context(nc.sbuf_tensor(n, list(sh), dt))

        def PSUM(n, sh, dt=F32):
            return stk.enter_context(nc.psum_tensor(n, list(sh), dt))

        x_t = S("x_t", [128, 9 * D], F32)
        xnf = S("xnf", [128, KD * TCE], BF16)
        stats = S("stats", [128, 64], F32)
        scr_tm = S("scr_tm", [128, D], BF16)
        wslab = S("wslab", [128, 16 * 512], BF16)
        ev = S("ev", [128, 512], F32)
        evb = S("evb", [128, 1024], BF16)
        sqb = S("sqb", [128, 512], BF16)
        rrow = S("rrow", [1, 512], F32)
        rrowb = S("rrowb", [1, 512], BF16)
        onescf = S("onescf", [128, 1], F32)
        onescb = S("onescb", [128, 1], BF16)
        onesrb = S("onesrb", [1, 128], BF16)
        onesrf = S("onesrf", [1, 128], F32)
        identb = S("identb", [128, 128], BF16)
        identf = S("identf", [128, 128], F32)
        trif = S("trif", [128, 128], F32)
        exp8b = S("exp8b", [8, 128], BF16)
        nrmw = S("nrmw", [N0, 2], F32)
        bias_nc = S("bias_nc", [N0, 2 * H0], F32)
        dtlam_tm = S("dtlam_tm", [128, 9 * 128], F32)
        dt_fm = S("dt_fm", [64, TCE], F32)
        lm_fm = S("lm_fm", [64, TCE], F32)
        la_tm = S("la_tm", [128, H0], F32)
        lam_tm = S("lam_tm", [128, 8 * H0], F32)
        lam_off = S("lam_off", [1, H0], F32)
        thtot = S("thtot", [1, 512], F32)
        Abct = S("Abct", [128, H0], F32)
        thdt = S("thdt", [128, 512], F32)
        gbcol = S("gbcol", [8, 1], F32)
        ffbias = S("ffbias", [128, D], F32)
        lf = S("lf", [8, TC], F32)
        ltm = S("ltm", [128, 16], F32)
        wtm = S("wtm", [128, 8], BF16)
        wfm = S("wfm", [8, TC], BF16)
        xAs = S("xAs", [128, TC], F32)
        xAw = S("xAw", [128, TC], BF16)
        bstk_sb = S("bstk_sb", [128, 2048], BF16)
        cgtmp = S("cgtmp", [64, TC], F32)

        pT = PSUM("pT", [128, 128], BF16)
        pTf = PSUM("pTf", [128, 128], F32)
        p0 = PSUM("p0", [128, 512], F32)
        p1 = PSUM("p1", [128, 512], F32)
        prow = PSUM("prow", [1, 512], F32)
        pbc = PSUM("pbc", [128, 512], F32)
        pth = PSUM("pth", [128, 512], F32)

        sems = {n: stk.enter_context(nc.semaphore(n)) for n in Sched.SEMS}
        blk = stk.enter_context(nc.Block())

        cpool = S("cpool", [128, 8], F32)
        for i, val in enumerate((1e-5, EPS0, 1.0, 0.0)):
            sch.v(lambda e, i=i, val=val: e.memset(cpool[:, i:i + 1], float(val)))
            nc.const_aps.register(F32, float(val), cpool[:, i:i + 1])

        xr = x_in.rearrange("(c p) d -> c p d", p=128)
        for c in range(9):
            sch.dma(x_t[:, c * D:(c + 1) * D], xr[c])
        sch.dma(onescf[:], CONST["ones_col"])
        sch.dma(onescb[:], CONST["ones_col"], cast=True)
        sch.dma(onesrb[:], CONST["ones_row"], cast=True)
        sch.dma(onesrf[:], CONST["ones_row"])
        sch.dma(identb[:], CONST["ident"], cast=True)
        sch.dma(identf[:], CONST["ident"])
        sch.dma(trif[:], CONST["tri_incl"])
        sch.dma(exp8b[:], CONST["exp8"], cast=True)
        sch.dma(nrmw[:], nw_in)
        sch.dma(bias_nc[:], bias_in)
        sch.dma(gbcol[:], gb)

        for c in range(9):
            xt = x_t[:, c * D:(c + 1) * D]
            sch.s(lambda e, xt=xt, c=c: e.activation(out=scr_tm[:], in_=xt, func=AF.Copy,
                                                     bias=0.0, scale=1.0,
                                                     accum_out=stats[:, 2 * c:2 * c + 1]))
            sch.s(lambda e, xt=xt, c=c: e.activation(out=scr_tm[:], in_=xt, func=AF.Square,
                                                     accum_out=stats[:, 2 * c + 1:2 * c + 2]))
            sch.v(lambda e, c=c: e.tensor_scalar(out=stats[:, 20 + c:21 + c],
                                                 in0=stats[:, 2 * c:2 * c + 1],
                                                 scalar1=1.0 / D, scalar2=None,
                                                 op0=OP.mult, op1=OP.bypass))
            sch.v(lambda e, c=c: e.tensor_scalar(out=stats[:, 30 + c:31 + c],
                                                 in0=stats[:, 2 * c + 1:2 * c + 2],
                                                 scalar1=1.0 / D, scalar2=None,
                                                 op0=OP.mult, op1=OP.bypass))
            sch.v(lambda e, c=c: e.tensor_tensor(out=stats[:, 50 + c:51 + c],
                                                 in0=stats[:, 20 + c:21 + c],
                                                 in1=stats[:, 20 + c:21 + c], op=OP.mult))
            sch.v(lambda e, c=c: e.tensor_tensor(out=stats[:, 30 + c:31 + c],
                                                 in0=stats[:, 30 + c:31 + c],
                                                 in1=stats[:, 50 + c:51 + c], op=OP.subtract))
            sch.s(lambda e, c=c: e.activation(out=stats[:, 50 + c:51 + c],
                                              in_=stats[:, 30 + c:31 + c],
                                              func=AF.Sqrt, bias=1e-5, scale=1.0))
            sch.v(lambda e, c=c: e.reciprocal(out=stats[:, 40 + c:41 + c],
                                              in_=stats[:, 50 + c:51 + c]))
            sch.v(lambda e, c=c: e.tensor_scalar(out=scr_tm[:], in0=x_t[:, c * D:(c + 1) * D],
                                                 scalar1=stats[:, 20 + c:21 + c],
                                                 scalar2=stats[:, 40 + c:41 + c],
                                                 op0=OP.subtract, op1=OP.mult))
            for k in range(KD):
                sch.t(lambda e, k=k: e.transpose(pT[:], scr_tm[:, k * 128:(k + 1) * 128],
                                                 identb[:]))
                sch.v(lambda e, c=c, k=k: e.tensor_copy(
                    out=xnf[:, k * TCE + c * 128:k * TCE + (c + 1) * 128], in_=pT[:]))
        for k in range(KD):
            sch.dma(o_xnf[k], xnf[:, k * TCE:k * TCE + TC])

        def load_slab(wdram_og):
            wr = wdram_og.rearrange("(k p) o -> k p o", p=128)
            for k in range(KD):
                sch.dma(wslab[:, k * 512:(k + 1) * 512], wr[k], cast=True)

        def mm_acc(ptile, lhs_off, tt0, nfree):
            for k in range(KD):
                sch.t(lambda e, k=k, lhs_off=lhs_off, tt0=tt0, nfree=nfree:
                      e.matmul(ptile[:, 0:nfree],
                               wslab[:, k * 512 + lhs_off:k * 512 + lhs_off + 128],
                               xnf[:, k * TCE + tt0:k * TCE + tt0 + nfree],
                               start=(k == 0), stop=(k == KD - 1)))

        def mm_acc_tm(ptile, tchunk, w_off, nfree):
            for k in range(KD):
                sch.t(lambda e, k=k, tchunk=tchunk, w_off=w_off, nfree=nfree:
                      e.matmul(ptile[:, 0:nfree],
                               xnf[:, k * TCE + tchunk * 128:k * TCE + (tchunk + 1) * 128],
                               wslab[:, k * 512 + w_off:k * 512 + w_off + nfree],
                               start=(k == 0), stop=(k == KD - 1)))

        for (wd, odst, wcol, bcol0) in ((WBt, o_Bn, 0, 0), (WCt, o_Cn, 1, H0)):
            for og in range(16):
                load_slab(wd[og])
                for i in range(4):
                    h = og * 4 + i
                    for tt in range(2):
                        mm_acc(p0, i * 128, tt * 512, 512)
                        sch.v(lambda e: e.tensor_copy(out=ev[:, 0:512], in_=p0[:, 0:512]))
                        sch.s(lambda e: e.activation(out=sqb[:], in_=p0[:, 0:512],
                                                     func=AF.Square))
                        sch.t(lambda e: e.matmul(prow[:, 0:512], onescb[:], sqb[:],
                                                 start=True, stop=True))
                        sch.s(lambda e: e.activation(out=rrow[:], in_=prow[:, 0:512],
                                                     func=AF.Sqrt, bias=EPS0, scale=1.0 / N0))
                        sch.v(lambda e: e.reciprocal(out=rrow[:], in_=rrow[:]))
                        sch.v(lambda e: e.tensor_copy(out=rrowb[:], in_=rrow[:]))
                        sch.t(lambda e: e.matmul(pbc[:, 0:512], onesrb[:], rrowb[:],
                                                 start=True, stop=True))
                        sch.v(lambda e: e.tensor_tensor(out=ev[:, 0:512], in0=ev[:, 0:512],
                                                        in1=pbc[:, 0:512], op=OP.mult))
                        sch.v(lambda e, h=h, wcol=wcol, bcol0=bcol0: e.tensor_scalar(
                            out=evb[:, 0:512], in0=ev[:, 0:512],
                            scalar1=nrmw[:, wcol:wcol + 1],
                            scalar2=bias_nc[:, bcol0 + h:bcol0 + h + 1],
                            op0=OP.mult, op1=OP.add))
                        sch.dma(odst[h][:, tt * 512:(tt + 1) * 512], evb[:, 0:512])

        for og in range(8):
            load_slab(Wxt[og])
            for tc_ in range(8):
                mm_acc_tm(p0, tc_, 0, 512)
                sch.v(lambda e: e.tensor_copy(out=evb[:, 0:512], in_=p0[:, 0:512]))
                sch.dma(o_xh[tc_ * 128:(tc_ + 1) * 128, og * 512:(og + 1) * 512],
                        evb[:, 0:512])

        for og in range(8):
            load_slab(Wzt[og])
            for tc_ in range(8):
                mm_acc_tm(p0, tc_, 0, 512)
                sch.s(lambda e: e.activation(out=evb[:, 0:512], in_=p0[:, 0:512],
                                             func=AF.Silu))
                sch.dma(o_z[tc_ * 128:(tc_ + 1) * 128, og * 512:(og + 1) * 512],
                        evb[:, 0:512])

        wdl = Wdlt.rearrange("(k p) o -> k p o", p=128)
        for k in range(KD):
            sch.dma(wslab[:, k * 512:k * 512 + 128], wdl[k], cast=True)
        for tc_ in range(9):
            mm_acc_tm(pth, tc_, 0, 128)
            sch.v(lambda e, tc_=tc_: e.tensor_copy(out=dtlam_tm[:, tc_ * 128:(tc_ + 1) * 128],
                                                   in_=pth[:, 0:128]))
            dtr = lambda tc_: dtlam_tm[:, tc_ * 128:tc_ * 128 + 64]
            sch.s(lambda e, tc_=tc_: e.activation(out=ev[:, 0:64], in_=dtr(tc_), func=AF.Abs))
            sch.s(lambda e: e.activation(out=ev[:, 64:128], in_=ev[:, 0:64],
                                         func=AF.Exp, bias=0.0, scale=-1.0))
            sch.s(lambda e: e.activation(out=ev[:, 64:128], in_=ev[:, 64:128],
                                         func=AF.Ln, bias=1.0, scale=1.0))
            sch.v(lambda e, tc_=tc_: e.tensor_scalar(out=ev[:, 0:64], in0=dtr(tc_),
                                                     scalar1=0.0, scalar2=None,
                                                     op0=OP.max, op1=OP.bypass))
            sch.v(lambda e, tc_=tc_: e.tensor_tensor(out=dtr(tc_), in0=ev[:, 0:64],
                                                     in1=ev[:, 64:128], op=OP.add))
            sch.s(lambda e, tc_=tc_: e.activation(
                out=dtlam_tm[:, tc_ * 128 + 64:(tc_ + 1) * 128],
                in_=dtlam_tm[:, tc_ * 128 + 64:(tc_ + 1) * 128], func=AF.Sigmoid))
            sch.t(lambda e, tc_=tc_: e.transpose(pTf[0:64, 0:128],
                                                 dtlam_tm[:, tc_ * 128:tc_ * 128 + 64],
                                                 identf[:]))
            sch.v(lambda e, tc_=tc_: e.tensor_copy(out=dt_fm[:, tc_ * 128:(tc_ + 1) * 128],
                                                   in_=pTf[0:64, 0:128]))
            sch.t(lambda e, tc_=tc_: e.transpose(pTf[0:64, 0:128],
                                                 dtlam_tm[:, tc_ * 128 + 64:(tc_ + 1) * 128],
                                                 identf[:]))
            sch.v(lambda e, tc_=tc_: e.tensor_copy(out=lm_fm[:, tc_ * 128:(tc_ + 1) * 128],
                                                   in_=pTf[0:64, 0:128]))

        ar_in = S("ar_in", [1, H0], F32)
        sch.dma(ar_in[:], A_log)
        sch.s(lambda e: e.activation(out=ar_in[:], in_=ar_in[:], func=AF.Exp))
        sch.v(lambda e: e.tensor_scalar(out=ar_in[:], in0=ar_in[:], scalar1=-1.0,
                                        scalar2=None, op0=OP.mult, op1=OP.bypass))
        sch.t(lambda e: e.matmul(pth[:, 0:H0], onesrf[:], ar_in[:], start=True, stop=True))
        sch.v(lambda e: e.tensor_copy(out=Abct[:], in_=pth[:, 0:H0]))
        sch.v(lambda e: e.memset(lam_off[:], 0.0))
        for tc_ in range(8):
            sch.v(lambda e, tc_=tc_: e.tensor_tensor(
                out=la_tm[:], in0=dtlam_tm[:, tc_ * 128:tc_ * 128 + 64],
                in1=Abct[:, 0:64], op=OP.mult))
            sch.t(lambda e: e.matmul(pth[:, 0:H0], trif[:], la_tm[:, 0:H0],
                                     start=True, stop=False))
            sch.t(lambda e: e.matmul(pth[:, 0:H0], onesrf[:], lam_off[:],
                                     start=False, stop=True))
            sch.v(lambda e, tc_=tc_: e.tensor_copy(out=lam_tm[:, tc_ * H0:(tc_ + 1) * H0],
                                                   in_=pth[:, 0:H0]))
            sch.t(lambda e: e.matmul(prow[:, 0:H0], onescf[:], la_tm[:, 0:H0],
                                     start=True, stop=True))
            sch.v(lambda e: e.tensor_tensor(out=lam_off[:], in0=lam_off[:],
                                            in1=prow[:, 0:H0], op=OP.add))
            sch.dma(o_lam_tm[tc_ * 128:(tc_ + 1) * 128, :], lam_tm[:, tc_ * H0:(tc_ + 1) * H0])
            sch.t(lambda e, tc_=tc_: e.transpose(pTf[0:64, 0:128], lam_tm[:, tc_ * H0:(tc_ + 1) * H0],
                                                 identf[:]))
            sch.v(lambda e, tc_=tc_: e.tensor_copy(out=ev[0:64, 0:128], in_=pTf[0:64, 0:128]))
            sch.dma(o_lam_fm[:, tc_ * 128:(tc_ + 1) * 128], ev[0:64, 0:128])

        sch.v(lambda e: e.tensor_tensor(out=cgtmp[:, 0:TC], in0=dt_fm[:, 0:TC],
                                        in1=lm_fm[:, 0:TC], op=OP.mult))
        sch.dma(o_cg_fm[64:128, :], cgtmp[:, 0:TC])
        sch.v(lambda e: e.tensor_scalar(out=xAs[0:64, 0:1024], in0=lm_fm[:, 1:TC + 1],
                                        scalar1=-1.0, scalar2=1.0, op0=OP.mult, op1=OP.add))
        sch.v(lambda e: e.tensor_tensor(out=xAs[0:64, 0:1024], in0=xAs[0:64, 0:1024],
                                        in1=dt_fm[:, 1:TC + 1], op=OP.mult))
        sch.v(lambda e: e.tensor_tensor(out=cgtmp[:, 0:TC], in0=cgtmp[:, 0:TC],
                                        in1=xAs[0:64, 0:1024], op=OP.add))
        sch.dma(o_cg_fm[0:64, :], cgtmp[:, 0:TC])

        for og in range(8):
            load_slab(Wtht[og])
            sch.v(lambda e: e.memset(thtot[:], 0.0))
            for tc_ in range(8):
                mm_acc_tm(p0, tc_, 0, 512)
                hh0 = og * 8
                dts = dtlam_tm[:, tc_ * 128 + hh0:tc_ * 128 + hh0 + 8]
                sch.v(lambda e, dts=dts: e.tensor_tensor(
                    out=thdt[:], in0=p0[:, 0:512],
                    in1=dts.unsqueeze(2).broadcast_to([128, 8, 64]), op=OP.mult))
                sch.t(lambda e: e.matmul(pth[:, 0:512], trif[:], thdt[:],
                                         start=True, stop=False))
                sch.t(lambda e: e.matmul(pth[:, 0:512], onesrf[:], thtot[:],
                                         start=False, stop=True))
                sch.v(lambda e: e.tensor_copy(out=ev[:, 0:512], in_=pth[:, 0:512]))
                sch.t(lambda e: e.matmul(prow[:, 0:512], onescf[:], thdt[:],
                                         start=True, stop=True))
                sch.v(lambda e: e.tensor_tensor(
                    out=thtot[:], in0=thtot[:], in1=prow[:, 0:512], op=OP.add))
                for i in range(4):
                    sch.t(lambda e, i=i: e.transpose(pTf[:], ev[:, i * 128:(i + 1) * 128],
                                                     identf[:]))
                    sch.v(lambda e, i=i: e.tensor_copy(out=thdt[:, i * 128:(i + 1) * 128],
                                                       in_=pTf[:]))
                    sch.dma(o_thc[og * 512 + i * 128:og * 512 + (i + 1) * 128,
                                  tc_ * 128:(tc_ + 1) * 128],
                            thdt[:, i * 128:(i + 1) * 128])
            sch.dma(o_tht[0:1, og * 512:(og + 1) * 512], thtot[:])


        for q in range(4):
            sch.dma(rrow[:], ffb[0:1, q * 512:(q + 1) * 512])
            sch.t(lambda e, q=q: e.matmul(pbc[:, 0:512], onesrf[:], rrow[:],
                                          start=True, stop=True))
            sch.v(lambda e, q=q: e.tensor_copy(out=ffbias[:, q * 512:(q + 1) * 512],
                                               in_=pbc[:, 0:512]))
        for c in range(8):
            sch.v(lambda e, c=c: e.tensor_tensor(out=x_t[:, c * D:(c + 1) * D],
                                                 in0=x_t[:, c * D:(c + 1) * D],
                                                 in1=ffbias[:], op=OP.add))
        for og in range(4):
            load_slab(Wfft[og])
            for tc_ in range(8):
                mm_acc_tm(p0, tc_, 0, 512)
                sch.v(lambda e, tc_=tc_, og=og: e.tensor_tensor(
                    out=x_t[:, tc_ * D + og * 512:tc_ * D + (og + 1) * 512],
                    in0=x_t[:, tc_ * D + og * 512:tc_ * D + (og + 1) * 512],
                    in1=p0[:, 0:512], op=OP.add))

        wgl = gwT.rearrange("(k p) o -> k p o", p=128)
        for k in range(KD):
            sch.dma(wslab[:, k * 512:k * 512 + 8], wgl[k], cast=True)
        for tt in range(2):
            for k in range(KD):
                sch.t(lambda e, k=k, tt=tt: e.matmul(p0[0:8, 0:512],
                                                     wslab[:, k * 512:k * 512 + 8],
                                                     xnf[:, k * TCE + tt * 512:k * TCE + (tt + 1) * 512],
                                                     start=(k == 0), stop=(k == KD - 1)))
            sch.v(lambda e, tt=tt: e.tensor_scalar(out=lf[:, tt * 512:(tt + 1) * 512],
                                                   in0=p0[0:8, 0:512], scalar1=gbcol[:],
                                                   scalar2=None, op0=OP.add, op1=OP.bypass))
        for tc_ in range(8):
            sch.t(lambda e, tc_=tc_: e.transpose(pTf[:, 0:8],
                                                 lf[:, tc_ * 128:(tc_ + 1) * 128],
                                                 identf[0:8, 0:8]))
            sch.v(lambda e: e.tensor_copy(out=ltm[:, 0:8], in_=pTf[:, 0:8]))
            sch.v(lambda e: e.reduce_max(out=stats[:, 60:61], in_=ltm[:, 0:8],
                                         axis=mybir.AxisListType.X))
            sch.v(lambda e: e.tensor_scalar(out=ltm[:, 8:16], in0=ltm[:, 0:8],
                                            scalar1=stats[:, 60:61], scalar2=None,
                                            op0=OP.is_equal, op1=OP.bypass))
            sch.v(lambda e: e.scalar_tensor_tensor(out=ltm[:, 8:16], in0=ltm[:, 8:16],
                                                   scalar=-1e30, in1=ltm[:, 0:8],
                                                   op0=OP.mult, op1=OP.add))
            sch.v(lambda e: e.reduce_max(out=stats[:, 61:62], in_=ltm[:, 8:16],
                                         axis=mybir.AxisListType.X))
            sch.v(lambda e: e.tensor_scalar(out=ltm[:, 8:16], in0=ltm[:, 0:8],
                                            scalar1=stats[:, 61:62], scalar2=None,
                                            op0=OP.is_ge, op1=OP.bypass))
            sch.v(lambda e: e.tensor_scalar(out=stats[:, 62:63], in0=stats[:, 60:61],
                                            scalar1=-1.0, scalar2=None,
                                            op0=OP.mult, op1=OP.bypass))
            sch.s(lambda e: e.activation(out=ltm[:, 0:8], in_=ltm[:, 0:8], func=AF.Exp,
                                         bias=stats[:, 62:63], scale=1.0))
            sch.s(lambda e: e.activation(out=stats[:, 63:64], in_=stats[:, 61:62],
                                         func=AF.Exp, bias=stats[:, 62:63], scale=1.0))
            sch.v(lambda e: e.tensor_scalar(out=stats[:, 63:64], in0=stats[:, 63:64],
                                            scalar1=1.0, scalar2=None, op0=OP.add,
                                            op1=OP.bypass))
            sch.v(lambda e: e.reciprocal(out=stats[:, 63:64], in_=stats[:, 63:64]))
            sch.v(lambda e: e.tensor_scalar(out=ltm[:, 0:8], in0=ltm[:, 0:8],
                                            scalar1=stats[:, 63:64], scalar2=None,
                                            op0=OP.mult, op1=OP.bypass))
            sch.v(lambda e: e.tensor_tensor(out=wtm[:, 0:8], in0=ltm[:, 0:8],
                                            in1=ltm[:, 8:16], op=OP.mult))
            sch.t(lambda e: e.transpose(pT[0:8, 0:128], wtm[:, 0:8], identb[:]))
            sch.v(lambda e, tc_=tc_: e.tensor_copy(out=wfm[:, tc_ * 128:(tc_ + 1) * 128],
                                                   in_=pT[0:8, 0:128]))

        wal = Aall.rearrange("(k p) o -> k p o", p=128)
        for k in range(KD):
            sch.dma(wslab[:, k * 512:k * 512 + 128], wal[k], cast=True)
        for tt in range(2):
            mm_acc(p0, 0, tt * 512, 512)
            sch.v(lambda e, tt=tt: e.tensor_copy(out=xAs[:, tt * 512:(tt + 1) * 512],
                                                 in_=p0[:, 0:512]))
        sch.dma(bstk_sb[:], Bstk, cast=True)
        for tc_ in range(8):
            sch.t(lambda e, tc_=tc_: e.matmul(pbc[:, 0:128], exp8b[:],
                                              wfm[:, tc_ * 128:(tc_ + 1) * 128],
                                              start=True, stop=True))
            sch.v(lambda e, tc_=tc_: e.tensor_tensor(out=xAw[:, tc_ * 128:(tc_ + 1) * 128],
                                                     in0=xAs[:, tc_ * 128:(tc_ + 1) * 128],
                                                     in1=pbc[:, 0:128], op=OP.mult))
        for tc_ in range(8):
            for og in range(4):
                sch.t(lambda e, tc_=tc_, og=og: e.matmul(
                    p1[:, 0:512], xAw[:, tc_ * 128:(tc_ + 1) * 128],
                    bstk_sb[:, og * 512:(og + 1) * 512], start=True, stop=True))
                sch.v(lambda e, tc_=tc_, og=og: e.tensor_tensor(
                    out=x_t[:, tc_ * D + og * 512:tc_ * D + (og + 1) * 512],
                    in0=x_t[:, tc_ * D + og * 512:tc_ * D + (og + 1) * 512],
                    in1=p1[:, 0:512], op=OP.add))

        for c in range(8):
            sch.dma(o_accx[c * 128:(c + 1) * 128, :], x_t[:, c * D:(c + 1) * D])

        sch.build(blk, sems)
    return nc



def prep_A_weights(W):
    f = np.float32
    p = _perm_n()
    WB = W["L0_attn_WB"].reshape(H0, N0, D)[:, p, :].reshape(H0 * N0, D).T
    WC = W["L0_attn_WC"].reshape(H0, N0, D)[:, p, :].reshape(H0 * N0, D).T
    out = {
        "WBt": prep_og(np.ascontiguousarray(WB)),
        "WCt": prep_og(np.ascontiguousarray(WC)),
        "Wxt": prep_og(np.ascontiguousarray(W["L0_attn_Wx"].T)),
        "Wzt": prep_og(np.ascontiguousarray(W["L0_attn_Wz"].T)),
        "Wtht": prep_og(np.ascontiguousarray(W["L0_attn_Wth"].T)),
        "Wdlt": np.ascontiguousarray(
            np.concatenate([W["L0_attn_Wdt"], W["L0_attn_Wlam"]], axis=0).T).astype(f),
        "Wfft": prep_og(np.ascontiguousarray(W["L0_ffn_W"].T)),
        "Aall": np.ascontiguousarray(W["L0_ffn_lora_A"].reshape(128, D).T).astype(f),
        "Bstk": np.ascontiguousarray(
            W["L0_ffn_lora_B"].transpose(0, 2, 1).reshape(128, D)).astype(f),
        "gwT": np.ascontiguousarray(W["L0_ffn_gate_w"].T).astype(f),
        "gb": W["L0_ffn_gate_b"].reshape(8, 1).astype(f),
        "ffb": W["L0_ffn_b"].reshape(1, -1).astype(f),
        "A_log": W["L0_attn_A_log"].reshape(1, H0).astype(f),
        "nw_in": np.stack([W["L0_attn_Bn_w"][p], W["L0_attn_Cn_w"][p]], axis=1).astype(f),
        "bias_in": np.concatenate([W["L0_attn_B_bias"][:, p].T,
                                   W["L0_attn_C_bias"][:, p].T], axis=1).astype(f),
    }
    for k, v in host_consts().items():
        out["c_" + k] = v
    return out


def run_A(inputs):
    x = np.asarray(inputs["L0_attn_x"]).reshape(T, D).astype(np.float32)
    wmap = prep_A_weights(inputs)
    nc = build_auto(build_A)
    in_maps = []
    for r in range(NCORES):
        xe = np.zeros((TCE, D), np.float32)
        lo = r * TC
        n = min(TC + 1, T - lo)
        xe[:n] = x[lo:lo + n]
        m = dict(wmap)
        m["x_ext"] = xe
        in_maps.append(m)
    res = run_bass_kernel_spmd(nc, in_maps, core_ids=list(range(NCORES)))
    return res.results




def build_B():
    nc = bass.Bass("TRN2", target_bir_lowering=False, name="mkB")

    def DR(n, sh, dt, k=None):
        return nc.dram_tensor(n, list(sh), dt, kind=k).ap() if k else nc.dram_tensor(n, list(sh), dt).ap()

    i_Bn = DR("i_Bn", (64, 128, TC), BF16, "ExternalInput")
    i_Cn = DR("i_Cn", (64, 128, TC), BF16, "ExternalInput")
    i_xh = DR("i_xh", (TC, 4096), BF16, "ExternalInput")
    i_thc = DR("i_thc", (4096, TC), F32, "ExternalInput")
    i_carry = DR("i_carry", (4096, 1), F32, "ExternalInput")
    i_lam_tm = DR("i_lam_tm", (TC, H0), F32, "ExternalInput")
    i_lam_fm = DR("i_lam_fm", (H0, TC), F32, "ExternalInput")
    i_cg_fm = DR("i_cg_fm", (128, TC), F32, "ExternalInput")
    HC = host_consts()
    CONST = {k: DR("c_" + k, v.shape, F32, "ExternalInput") for k, v in HC.items()}

    o_ypre = DR("o_ypre", (TC, 4096), BF16, "ExternalOutput")
    o_Ct = DR("o_Ct", (64, 128, TC), BF16, "ExternalOutput")
    o_hout = DR("o_hout", (64, 128, P0), F32, "ExternalOutput")

    TWO_PI = float(2.0 * np.pi)
    INV_2PI = float(1.0 / (2.0 * np.pi))
    RC = 12582912.0

    sch = Sched()
    with ExitStack() as stk:
        def S(n, sh, dt):
            return stk.enter_context(nc.sbuf_tensor(n, list(sh), dt))

        xh_sb = S("xh_sb", [128, 8 * 4096], BF16)
        lam_tm_sb = S("lam_tm_sb", [128, 8 * H0], F32)
        lam_fm_sb = S("lam_fm_sb", [H0, TC], F32)
        cg_fm_sb = S("cg_fm_sb", [128, TC], F32)
        lhat_tm = S("lhat_tm", [128, 8 * H0], F32)
        nlhat_tm = S("nlhat_tm", [128, 8 * H0], F32)
        elam = S("elam", [128, 8 * H0], F32)
        cg_tm = S("cg_tm", [128, 8 * 128], F32)
        dgc = S("dgc", [128, 8 * H0], F32)
        eEnd = S("eEnd", [64, 8], F32)
        offr = S("offr", [1, 128], F32)
        ang = S("ang", [64, TC], F32)
        ang2 = S("ang2", [64, TC], F32)
        angn = S("angn", [64, TC], F32)
        sinb = S("sinb", [64, TC], BF16)
        cosb = S("cosb", [64, TC], BF16)
        carr = S("carr", [64, 1], F32)
        Btp = S("Btp", [64, TC], BF16)
        Bbt = S("Bbt", [64, TC], BF16)
        Ctp = S("Ctp", [64, TC], BF16)
        Cbt = S("Cbt", [64, TC], BF16)
        rt1 = S("rt1", [64, TC], BF16)
        rt2 = S("rt2", [64, TC], BF16)
        RBt = S("RBt", [64, TC], BF16)
        RBb = S("RBb", [64, TC], BF16)
        RCt = S("RCt", [64, TC], BF16)
        RCb = S("RCb", [64, TC], BF16)
        Btm = S("Btm", [128, 128], BF16)
        Wm = S("Wm", [128, 128], BF16)
        WG = S("WG", [128, 128], BF16)
        argt = S("argt", [128, 128], F32)
        Et = S("Et", [128, 128], F32)
        lrow = S("lrow", [1, 128], F32)
        wEndc = S("wEndc", [128, 1], F32)
        ytmp = S("ytmp", [128, 64], BF16)
        ysb = S("ysb", [128, 64], BF16)
        htop = S("htop", [64, 64 * 64], F32)
        hbot = S("hbot", [64, 64 * 64], F32)
        htopb = S("htopb", [64, 64 * 64], BF16)
        hbotb = S("hbotb", [64, 64 * 64], BF16)
        onesrf = S("onesrf", [1, 128], F32)
        identb = S("identb", [128, 128], BF16)
        identf = S("identf", [128, 128], F32)
        maskst = S("maskst", [128, 128], F32)
        cpool = S("cpool", [128, 8], F32)

        pG = stk.enter_context(nc.psum_tensor("pG", [128, 128], F32))
        pO = stk.enter_context(nc.psum_tensor("pO", [128, 64], F32))
        pI = stk.enter_context(nc.psum_tensor("pI", [128, 64], F32))
        pT1 = stk.enter_context(nc.psum_tensor("pT1", [128, 128], F32))
        pTr = stk.enter_context(nc.psum_tensor("pTr", [128, 128], F32))
        pTb = stk.enter_context(nc.psum_tensor("pTb", [128, 128], BF16))
        pS = stk.enter_context(nc.psum_tensor("pS", [64, 64], F32))

        sems = {n: stk.enter_context(nc.semaphore(n)) for n in Sched.SEMS}
        blk = stk.enter_context(nc.Block())

        for i, val in enumerate((EPS0, 1.0, 0.0)):
            sch.v(lambda e, i=i, val=val: e.memset(cpool[:, i:i + 1], float(val)))
            nc.const_aps.register(F32, float(val), cpool[:, i:i + 1])

        xr = i_xh.rearrange("(c p) d -> c p d", p=128)
        for c in range(8):
            sch.dma(xh_sb[:, c * 4096:(c + 1) * 4096], xr[c])
        lr = i_lam_tm.rearrange("(c p) h -> c p h", p=128)
        for c in range(8):
            sch.dma(lam_tm_sb[:, c * H0:(c + 1) * H0], lr[c])
        sch.dma(lam_fm_sb[:], i_lam_fm)
        sch.dma(cg_fm_sb[:], i_cg_fm)
        sch.dma(onesrf[:], CONST["ones_row"])
        sch.dma(identb[:], CONST["ident"], cast=True)
        sch.dma(identf[:], CONST["ident"])
        sch.dma(maskst[:], CONST["mask_incl_add"])

        for c in range(8):
            t0 = c * 128
            if c == 0:
                sch.v(lambda e: e.memset(offr[:], 0.0))
            else:
                sch.t(lambda e, t0=t0: e.transpose(pTr[0:1, 0:64],
                                                   lam_fm_sb[:, t0 - 1:t0], identf[0:64, 0:64]))
                sch.v(lambda e: e.tensor_copy(out=offr[0:1, 0:64], in_=pTr[0:1, 0:64]))
            sch.t(lambda e: e.matmul(pT1[:, 0:64], onesrf[:], offr[0:1, 0:64],
                                     start=True, stop=True))
            sch.v(lambda e, c=c: e.tensor_tensor(out=lhat_tm[:, c * H0:(c + 1) * H0],
                                                 in0=lam_tm_sb[:, c * H0:(c + 1) * H0],
                                                 in1=pT1[:, 0:64], op=OP.subtract))
            sch.v(lambda e, c=c: e.tensor_scalar(out=nlhat_tm[:, c * H0:(c + 1) * H0],
                                                 in0=lhat_tm[:, c * H0:(c + 1) * H0],
                                                 scalar1=-1.0, scalar2=None,
                                                 op0=OP.mult, op1=OP.bypass))
            sch.s(lambda e, c=c: e.activation(out=elam[:, c * H0:(c + 1) * H0],
                                              in_=lhat_tm[:, c * H0:(c + 1) * H0],
                                              func=AF.Exp))
            sch.t(lambda e, c=c: e.transpose(pTr[0:64, 0:128],
                                             elam[:, c * H0:(c + 1) * H0], identf[:]))
            sch.v(lambda e, c=c: e.tensor_copy(out=eEnd[:, c:c + 1], in_=pTr[0:64, 127:128]))
            sch.t(lambda e, t0=t0: e.transpose(pTr[:], cg_fm_sb[:, t0:t0 + 128], identf[:]))
            sch.v(lambda e, c=c: e.tensor_copy(out=cg_tm[:, c * 128:(c + 1) * 128], in_=pTr[:]))
            sch.v(lambda e, c=c: e.tensor_tensor(out=dgc[:, c * H0:(c + 1) * H0],
                                                 in0=cg_tm[:, c * 128 + 64:(c + 1) * 128],
                                                 in1=cg_tm[:, c * 128:c * 128 + 64],
                                                 op=OP.subtract))

        for h in range(H0):
            sch.dma(ang[:], i_thc[h * 64:(h + 1) * 64, :])
            sch.dma(carr[:], i_carry[h * 64:(h + 1) * 64, :])
            sch.v(lambda e: e.tensor_scalar(out=ang[:], in0=ang[:], scalar1=carr[:],
                                            scalar2=INV_2PI, op0=OP.add, op1=OP.mult))
            sch.v(lambda e: e.tensor_scalar(out=angn[:], in0=ang[:], scalar1=RC,
                                            scalar2=RC, op0=OP.add, op1=OP.subtract))
            sch.v(lambda e: e.tensor_tensor(out=ang2[:], in0=ang[:], in1=angn[:],
                                            op=OP.subtract))
            sch.v(lambda e: e.tensor_scalar(out=ang2[:], in0=ang2[:], scalar1=TWO_PI,
                                            scalar2=None, op0=OP.mult, op1=OP.bypass))
            sch.s(lambda e: e.activation(out=sinb[:], in_=ang2[:], func=AF.Sin))
            sch.v(lambda e: e.tensor_scalar(out=ang[:], in0=ang[:], scalar1=0.25,
                                            scalar2=None, op0=OP.add, op1=OP.bypass))
            sch.v(lambda e: e.tensor_scalar(out=angn[:], in0=ang[:], scalar1=RC,
                                            scalar2=RC, op0=OP.add, op1=OP.subtract))
            sch.v(lambda e: e.tensor_tensor(out=ang2[:], in0=ang[:], in1=angn[:],
                                            op=OP.subtract))
            sch.v(lambda e: e.tensor_scalar(out=ang2[:], in0=ang2[:], scalar1=TWO_PI,
                                            scalar2=None, op0=OP.mult, op1=OP.bypass))
            sch.s(lambda e: e.activation(out=cosb[:], in_=ang2[:], func=AF.Sin))
            sch.dma(Btp[:], i_Bn[h][0:64, :])
            sch.dma(Bbt[:], i_Bn[h][64:128, :])
            sch.dma(Ctp[:], i_Cn[h][0:64, :])
            sch.dma(Cbt[:], i_Cn[h][64:128, :])
            for (top, bot, rtop, rbot) in ((Btp, Bbt, RBt, RBb), (Ctp, Cbt, RCt, RCb)):
                sch.v(lambda e, bot=bot: e.tensor_tensor(out=rt1[:], in0=bot[:], in1=sinb[:],
                                                         op=OP.mult))
                sch.v(lambda e, top=top: e.tensor_tensor(out=rt2[:], in0=top[:], in1=cosb[:],
                                                         op=OP.mult))
                sch.v(lambda e, rtop=rtop: e.tensor_tensor(out=rtop[:], in0=rt2[:], in1=rt1[:],
                                                           op=OP.subtract))
                sch.v(lambda e, top=top: e.tensor_tensor(out=rt1[:], in0=top[:], in1=sinb[:],
                                                         op=OP.mult))
                sch.v(lambda e, bot=bot: e.tensor_tensor(out=rt2[:], in0=bot[:], in1=cosb[:],
                                                         op=OP.mult))
                sch.v(lambda e, rbot=rbot: e.tensor_tensor(out=rbot[:], in0=rt2[:], in1=rt1[:],
                                                           op=OP.add))
            sch.dma(o_Ct[h][0:64, :], RCt[:])
            sch.dma(o_Ct[h][64:128, :], RCb[:])
            for t_ in (htop, hbot):
                sch.v(lambda e, t_=t_, h=h: e.memset(t_[:, h * 64:(h + 1) * 64], 0.0))
            for t_ in (htopb, hbotb):
                sch.v(lambda e, t_=t_, h=h: e.memset(t_[:, h * 64:(h + 1) * 64], 0.0))

            for c in range(8):
                t0 = c * 128
                sch.t(lambda e, c=c, h=h: e.transpose(pTr[0:1, 0:128],
                                                      lhat_tm[:, c * H0 + h:c * H0 + h + 1],
                                                      identf[:]))
                sch.v(lambda e: e.tensor_copy(out=lrow[:], in_=pTr[0:1, 0:128]))
                sch.t(lambda e: e.matmul(pT1[:], onesrf[:], lrow[:], start=True, stop=True))
                sch.v(lambda e: e.scalar_tensor_tensor(out=argt[:], in0=pT1[:], scalar=1.0,
                                                       in1=maskst[:], op0=OP.mult, op1=OP.add))
                sch.s(lambda e, c=c, h=h: e.activation(out=Et[:], in_=argt[:], func=AF.Exp,
                                                       bias=nlhat_tm[:, c * H0 + h:c * H0 + h + 1],
                                                       scale=1.0))
                sch.v(lambda e, c=c, h=h: e.tensor_scalar(out=Wm[:], in0=Et[:],
                                                          scalar1=cg_tm[:, c * 128 + h:c * 128 + h + 1],
                                                          scalar2=None, op0=OP.mult, op1=OP.bypass))
                sch.v(lambda e, c=c, h=h: e.scalar_tensor_tensor(
                    out=Wm[:], in0=identb[:], scalar=dgc[:, c * H0 + h:c * H0 + h + 1],
                    in1=Wm[:], op0=OP.mult, op1=OP.add))
                sch.t(lambda e, t0=t0: e.matmul(pG[:], RBt[:, t0:t0 + 128], RCt[:, t0:t0 + 128],
                                                start=True, stop=False))
                sch.t(lambda e, t0=t0: e.matmul(pG[:], RBb[:, t0:t0 + 128], RCb[:, t0:t0 + 128],
                                                start=False, stop=True))
                sch.v(lambda e: e.tensor_tensor(out=WG[:], in0=Wm[:], in1=pG[:], op=OP.mult))
                sch.t(lambda e, c=c, h=h: e.matmul(pO[:], WG[:],
                                                   xh_sb[:, c * 4096 + h * 64:c * 4096 + (h + 1) * 64],
                                                   start=True, stop=True))
                sch.v(lambda e: e.tensor_copy(out=ytmp[:], in_=pO[:]))
                sch.t(lambda e, t0=t0, h=h: e.matmul(pI[:], RCt[:, t0:t0 + 128],
                                                     htopb[:, h * 64:(h + 1) * 64],
                                                     start=True, stop=False))
                sch.t(lambda e, t0=t0, h=h: e.matmul(pI[:], RCb[:, t0:t0 + 128],
                                                     hbotb[:, h * 64:(h + 1) * 64],
                                                     start=False, stop=True))
                sch.v(lambda e, c=c, h=h: e.scalar_tensor_tensor(
                    out=ysb[:], in0=pI[:], scalar=elam[:, c * H0 + h:c * H0 + h + 1],
                    in1=ytmp[:], op0=OP.mult, op1=OP.add))
                sch.dma(o_ypre[t0:t0 + 128, h * 64:(h + 1) * 64], ysb[:])
                sch.v(lambda e, c=c, h=h: e.tensor_scalar(
                    out=wEndc[:], in0=Et[:, 127:128],
                    scalar1=cg_tm[:, c * 128 + h:c * 128 + h + 1],
                    scalar2=None, op0=OP.mult, op1=OP.bypass))
                for (rb, hf, hb) in ((RBt, htop, htopb), (RBb, hbot, hbotb)):
                    sch.t(lambda e, rb=rb, t0=t0: e.transpose(pTb[:, 0:64], rb[:, t0:t0 + 128],
                                                              identb[0:64, 0:64]))
                    sch.v(lambda e: e.tensor_scalar(out=Btm[:, 0:64], in0=pTb[:, 0:64],
                                                    scalar1=wEndc[:], scalar2=None,
                                                    op0=OP.mult, op1=OP.bypass))
                    sch.t(lambda e, c=c, h=h: e.matmul(
                        pS[:], Btm[:, 0:64],
                        xh_sb[:, c * 4096 + h * 64:c * 4096 + (h + 1) * 64],
                        start=True, stop=True))
                    sch.v(lambda e, c=c, h=h, hf=hf: e.scalar_tensor_tensor(
                        out=hf[:, h * 64:(h + 1) * 64], in0=hf[:, h * 64:(h + 1) * 64],
                        scalar=eEnd[:, c:c + 1], in1=pS[:], op0=OP.mult, op1=OP.add))
                    sch.v(lambda e, h=h, hf=hf, hb=hb: e.tensor_copy(
                        out=hb[:, h * 64:(h + 1) * 64], in_=hf[:, h * 64:(h + 1) * 64]))
            sch.dma(o_hout[h][0:64, :], htop[:, h * 64:(h + 1) * 64])
            sch.dma(o_hout[h][64:128, :], hbot[:, h * 64:(h + 1) * 64])

        sch.build(blk, sems)
    return nc


def run_B(a_results, th_totals):
    nc = build_auto(build_B)
    HC = host_consts()
    in_maps = []
    for r in range(NCORES):
        a = a_results[r]
        carry = np.zeros((4096, 1), np.float32)
        for p in range(r):
            carry[:, 0] += th_totals[p].ravel()
        m = {"i_Bn": a["o_Bn"], "i_Cn": a["o_Cn"], "i_xh": a["o_xh"],
             "i_thc": a["o_thc"], "i_carry": carry, "i_lam_tm": a["o_lam_tm"],
             "i_lam_fm": a["o_lam_fm"], "i_cg_fm": a["o_cg_fm"]}
        for k, v in HC.items():
            m["c_" + k] = v
        in_maps.append(m)
    res = run_bass_kernel_spmd(nc, in_maps, core_ids=list(range(NCORES)))
    return res.results




def build_C():
    nc = bass.Bass("TRN2", target_bir_lowering=False, name="mkC")

    def DR(n, sh, dt, k=None):
        return nc.dram_tensor(n, list(sh), dt, kind=k).ap() if k else nc.dram_tensor(n, list(sh), dt).ap()

    i_ypre = DR("i_ypre", (TC, 4096), BF16, "ExternalInput")
    i_Ct = DR("i_Ct", (64, 128, TC), BF16, "ExternalInput")
    i_z = DR("i_z", (TC, 4096), BF16, "ExternalInput")
    i_accx = DR("i_accx", (TC, D), F32, "ExternalInput")
    i_lam_tm = DR("i_lam_tm", (TC, H0), F32, "ExternalInput")
    i_hprev = DR("i_hprev", (7, 64, 128, P0), F32, "ExternalInput")
    i_ltot = DR("i_ltot", (1, 512), F32, "ExternalInput")
    Woutt = DR("Woutt", (4, 4096, 512), F32, "ExternalInput")
    HC = host_consts()
    CONST = {k: DR("c_" + k, v.shape, F32, "ExternalInput") for k, v in HC.items()}

    o_x1 = DR("o_x1", (TC, D), F32, "ExternalOutput")
    o_xn1f = DR("o_xn1f", (16, 128, TC), BF16, "ExternalOutput")
    o_xn1tm = DR("o_xn1tm", (TC, D), BF16, "ExternalOutput")

    sch = Sched()
    with ExitStack() as stk:
        def S(n, sh, dt):
            return stk.enter_context(nc.sbuf_tensor(n, list(sh), dt))

        yg = S("yg", [128, 8 * 4096], BF16)
        accx = S("accx", [128, 8 * D], F32)
        lam_tm_sb = S("lam_tm_sb", [128, 8 * H0], F32)
        elam = S("elam", [128, H0], F32)
        Sintb = S("Sintb", [64, 64 * 64], BF16)
        Sinbb = S("Sinbb", [64, 64 * 64], BF16)
        hstg = S("hstg", [64, 64], F32)
        hstgb = S("hstgb", [64, 64], F32)
        ltot = S("ltot", [1, 512], F32)
        suf = S("suf", [1, 512], F32)
        drow = S("drow", [1, 64], F32)
        Dbc = S("Dbc", [128, 64], F32)
        Ctp = S("Ctp", [64, TC], BF16)
        Cbt = S("Cbt", [64, TC], BF16)
        ypc = S("ypc", [128, 64], BF16)
        zc = S("zc", [128, 64], BF16)
        wslab = S("wslab", [128, 32 * 512], BF16)
        evb = S("evb", [128, 512], BF16)
        scr_tm = S("scr_tm", [128, D], BF16)
        stats = S("stats", [128, 64], F32)
        onesrf = S("onesrf", [1, 128], F32)
        identb = S("identb", [128, 128], BF16)
        cpool = S("cpool", [128, 8], F32)

        pC = stk.enter_context(nc.psum_tensor("pC", [128, 64], F32))
        pW = stk.enter_context(nc.psum_tensor("pW", [128, 512], F32))
        pTb = stk.enter_context(nc.psum_tensor("pTb", [128, 128], BF16))
        pB = stk.enter_context(nc.psum_tensor("pB", [128, 64], F32))

        sems = {n: stk.enter_context(nc.semaphore(n)) for n in Sched.SEMS}
        blk = stk.enter_context(nc.Block())

        for i, val in enumerate((1e-5, 1.0, 0.0)):
            sch.v(lambda e, i=i, val=val: e.memset(cpool[:, i:i + 1], float(val)))
            nc.const_aps.register(F32, float(val), cpool[:, i:i + 1])

        yr = i_ypre.rearrange("(c p) d -> c p d", p=128)
        ar = i_accx.rearrange("(c p) d -> c p d", p=128)
        lr = i_lam_tm.rearrange("(c p) h -> c p h", p=128)
        for c in range(8):
            sch.dma(accx[:, c * D:(c + 1) * D], ar[c])
            sch.dma(lam_tm_sb[:, c * H0:(c + 1) * H0], lr[c])
        sch.dma(ltot[:], i_ltot)
        sch.dma(onesrf[:], CONST["ones_row"])
        sch.dma(identb[:], CONST["ident"], cast=True)

        sch.v(lambda e: e.memset(suf[0:1, 7 * 64:8 * 64], 0.0))
        for q in range(6, -1, -1):
            sch.v(lambda e, q=q: e.tensor_tensor(
                out=suf[0:1, q * 64:(q + 1) * 64],
                in0=suf[0:1, (q + 1) * 64:(q + 2) * 64],
                in1=ltot[0:1, (q + 1) * 64:(q + 2) * 64], op=OP.add))
        for t_ in (Sintb, Sinbb):
            sch.v(lambda e, t_=t_: e.memset(t_[:], 0.0))
        for p in range(7):
            sch.s(lambda e, p=p: e.activation(out=drow[:], in_=suf[0:1, p * 64:(p + 1) * 64],
                                              func=AF.Exp))
            sch.t(lambda e: e.matmul(pB[:, 0:64], onesrf[:], drow[:], start=True, stop=True))
            sch.v(lambda e: e.tensor_copy(out=Dbc[:], in_=pB[:, 0:64]))
            for h in range(H0):
                sch.dma(hstg[:], i_hprev[p][h][0:64, :])
                sch.dma(hstgb[:], i_hprev[p][h][64:128, :])
                sch.v(lambda e, h=h: e.scalar_tensor_tensor(
                    out=Sintb[:, h * 64:(h + 1) * 64], in0=hstg[:],
                    scalar=Dbc[0:64, h:h + 1], in1=Sintb[:, h * 64:(h + 1) * 64],
                    op0=OP.mult, op1=OP.add))
                sch.v(lambda e, h=h: e.scalar_tensor_tensor(
                    out=Sinbb[:, h * 64:(h + 1) * 64], in0=hstgb[:],
                    scalar=Dbc[0:64, h:h + 1], in1=Sinbb[:, h * 64:(h + 1) * 64],
                    op0=OP.mult, op1=OP.add))

        for h in range(H0):
            sch.dma(Ctp[:], i_Ct[h][0:64, :])
            sch.dma(Cbt[:], i_Ct[h][64:128, :])
            for c in range(8):
                t0 = c * 128
                sch.s(lambda e, c=c: e.activation(out=elam[:], in_=lam_tm_sb[:, c * H0:(c + 1) * H0],
                                                  func=AF.Exp))
                sch.t(lambda e, t0=t0, h=h: e.matmul(pC[:], Ctp[:, t0:t0 + 128],
                                                     Sintb[:, h * 64:(h + 1) * 64],
                                                     start=True, stop=False))
                sch.t(lambda e, t0=t0, h=h: e.matmul(pC[:], Cbt[:, t0:t0 + 128],
                                                     Sinbb[:, h * 64:(h + 1) * 64],
                                                     start=False, stop=True))
                sch.dma(ypc[:], yr[c][:, h * 64:(h + 1) * 64])
                sch.dma(zc[:], i_z[t0:t0 + 128, h * 64:(h + 1) * 64])
                sch.v(lambda e, h=h: e.scalar_tensor_tensor(
                    out=ypc[:], in0=pC[:], scalar=elam[:, h:h + 1], in1=ypc[:],
                    op0=OP.mult, op1=OP.add))
                sch.v(lambda e, c=c, h=h: e.tensor_tensor(
                    out=yg[:, c * 4096 + h * 64:c * 4096 + (h + 1) * 64],
                    in0=ypc[:], in1=zc[:], op=OP.mult))

        for c in range(8):
            for k in range(32):
                sl = yg[:, c * 4096 + k * 128:c * 4096 + (k + 1) * 128]
                sch.t(lambda e, sl=sl: e.transpose(pTb[:], sl, identb[:]))
                sch.v(lambda e, sl=sl: e.tensor_copy(out=sl, in_=pTb[:]))

        def load_slab(wdram_og, nk=8):
            wr = wdram_og.rearrange("(k p) o -> k p o", p=128)
            for k in range(nk):
                sch.dma(wslab[:, k * 512:(k + 1) * 512], wr[k], cast=True)

        for og in range(4):
            wr = Woutt[og].rearrange("(k p) o -> k p o", p=128)
            for k in range(32):
                sch.dma(wslab[:, k * 512:(k + 1) * 512], wr[k], cast=True)
            for tc_ in range(8):
                for k in range(32):
                    sch.t(lambda e, tc_=tc_, k=k, og=og: e.matmul(
                        pW[:, 0:512], yg[:, tc_ * 4096 + k * 128:tc_ * 4096 + (k + 1) * 128],
                        wslab[:, k * 512:(k + 1) * 512],
                        start=(k == 0), stop=(k == 31)))
                sch.v(lambda e, tc_=tc_, og=og: e.tensor_tensor(
                    out=accx[:, tc_ * D + og * 512:tc_ * D + (og + 1) * 512],
                    in0=accx[:, tc_ * D + og * 512:tc_ * D + (og + 1) * 512],
                    in1=pW[:, 0:512], op=OP.add))
        for c in range(8):
            sch.dma(o_x1[c * 128:(c + 1) * 128, :], accx[:, c * D:(c + 1) * D])

        for c in range(8):
            xt = accx[:, c * D:(c + 1) * D]
            sch.s(lambda e, xt=xt, c=c: e.activation(out=scr_tm[:], in_=xt, func=AF.Copy,
                                                     bias=0.0, scale=1.0,
                                                     accum_out=stats[:, 2 * c:2 * c + 1]))
            sch.s(lambda e, xt=xt, c=c: e.activation(out=scr_tm[:], in_=xt, func=AF.Square,
                                                     accum_out=stats[:, 2 * c + 1:2 * c + 2]))
            sch.v(lambda e, c=c: e.tensor_scalar(out=stats[:, 20 + c:21 + c],
                                                 in0=stats[:, 2 * c:2 * c + 1],
                                                 scalar1=1.0 / D, scalar2=None,
                                                 op0=OP.mult, op1=OP.bypass))
            sch.v(lambda e, c=c: e.tensor_scalar(out=stats[:, 30 + c:31 + c],
                                                 in0=stats[:, 2 * c + 1:2 * c + 2],
                                                 scalar1=1.0 / D, scalar2=None,
                                                 op0=OP.mult, op1=OP.bypass))
            sch.v(lambda e, c=c: e.tensor_tensor(out=stats[:, 50 + c:51 + c],
                                                 in0=stats[:, 20 + c:21 + c],
                                                 in1=stats[:, 20 + c:21 + c], op=OP.mult))
            sch.v(lambda e, c=c: e.tensor_tensor(out=stats[:, 30 + c:31 + c],
                                                 in0=stats[:, 30 + c:31 + c],
                                                 in1=stats[:, 50 + c:51 + c], op=OP.subtract))
            sch.s(lambda e, c=c: e.activation(out=stats[:, 50 + c:51 + c],
                                              in_=stats[:, 30 + c:31 + c],
                                              func=AF.Sqrt, bias=1e-5, scale=1.0))
            sch.v(lambda e, c=c: e.reciprocal(out=stats[:, 40 + c:41 + c],
                                              in_=stats[:, 50 + c:51 + c]))
            sch.v(lambda e, c=c, xt=xt: e.tensor_scalar(out=scr_tm[:], in0=xt,
                                                 scalar1=stats[:, 20 + c:21 + c],
                                                 scalar2=stats[:, 40 + c:41 + c],
                                                 op0=OP.subtract, op1=OP.mult))
            sch.dma(o_xn1tm[c * 128:(c + 1) * 128, :], scr_tm[:])
            for k in range(KD):
                sch.t(lambda e, k=k: e.transpose(pTb[:], scr_tm[:, k * 128:(k + 1) * 128],
                                                 identb[:]))
                sch.v(lambda e, c=c, k=k: e.tensor_copy(out=evb[:, 0:128], in_=pTb[:]))
                sch.dma(o_xn1f[k][:, c * 128:(c + 1) * 128], evb[:, 0:128])

        sch.build(blk, sems)
    return nc


def run_C(a_results, b_results):
    nc = build_auto(build_C)
    HC = host_consts()
    dat_w = None
    in_maps = []
    for r in range(NCORES):
        a, b = a_results[r], b_results[r]
        hprev = np.zeros((7, 64, 128, P0), np.float32)
        for p in range(min(r, 7)):
            hprev[p] = b_results[p]["o_hout"]
        ltot = np.zeros((1, 512), np.float32)
        for q in range(r):
            ltot[0, q * 64:(q + 1) * 64] = a_results[q]["o_lam_tm"][TC - 1, :]
        m = {"i_ypre": b["o_ypre"], "i_Ct": b["o_Ct"], "i_z": a["o_z"],
             "i_accx": a["o_accx"], "i_lam_tm": a["o_lam_tm"],
             "i_hprev": hprev, "i_ltot": ltot, "Woutt": run_C.Woutt}
        for k, v in HC.items():
            m["c_" + k] = v
        in_maps.append(m)
    res = run_bass_kernel_spmd(nc, in_maps, core_ids=list(range(NCORES)))
    return res.results




SQ_DK = float(1.0 / np.sqrt(128.0))


def build_D():
    nc = bass.Bass("TRN2", target_bir_lowering=False, name="mkD")

    def DR(n, sh, dt, k=None):
        return nc.dram_tensor(n, list(sh), dt, kind=k).ap() if k else nc.dram_tensor(n, list(sh), dt).ap()

    i_xnf = DR("i_xnf", (16, 128, TC), BF16, "ExternalInput")
    i_xb3 = DR("i_xb3", (3, D), BF16, "ExternalInput")
    Wqkv = DR("Wqkv", (16, D, 512), F32, "ExternalInput")
    Wz1 = DR("Wz1", (8, D, 512), F32, "ExternalInput")
    Wba = DR("Wba", (D, 64), F32, "ExternalInput")
    convw = DR("convw", (128, 256), F32, "ExternalInput")
    convb = DR("convb", (128, 64), F32, "ExternalInput")
    dtb1 = DR("dtb1", (1, VH), F32, "ExternalInput")
    Alog1 = DR("Alog1", (1, VH), F32, "ExternalInput")
    Wg1 = DR("Wg1", (11, D, 512), F32, "ExternalInput")
    Wu1 = DR("Wu1", (11, D, 512), F32, "ExternalInput")
    Wd1 = DR("Wd1", (4, INTER, 512), F32, "ExternalInput")
    gs1 = DR("gs1", (128, 44), F32, "ExternalInput")
    us1 = DR("us1", (128, 44), F32, "ExternalInput")
    ds1 = DR("ds1", (128, 16), F32, "ExternalInput")
    HC = host_consts()
    CONST = {k: DR("c_" + k, v.shape, F32, "ExternalInput") for k, v in HC.items()}

    o_Opre = DR("o_Opre", (TC, VDIM), BF16, "ExternalOutput")
    o_qhat = DR("o_qhat", (16, 128, TC), BF16, "ExternalOutput")
    o_ghat = DR("o_ghat", (TC, VH), F32, "ExternalOutput")
    o_McT = DR("o_McT", (8, VH, 128, 128), BF16, "ExternalOutput")
    o_MrT = DR("o_MrT", (VH, 128, 128), BF16, "ExternalOutput")
    o_Sloc = DR("o_Sloc", (VH, 128, 128), BF16, "ExternalOutput")
    o_zsig = DR("o_zsig", (32, 128, TC), BF16, "ExternalOutput")
    o_ffn1 = DR("o_ffn1", (TC, D), F32, "ExternalOutput")

    qkv_scr = DR("qkv_scr", (64, 128, TC), BF16)
    hdn_scr = DR("hdn_scr", (44, 128, TC), BF16)

    sch = Sched()
    with ExitStack() as stk:
        def S(n, sh, dt):
            return stk.enter_context(nc.sbuf_tensor(n, list(sh), dt))

        xnf = S("xnf", [128, 16 * TC], BF16)
        xb3f = S("xb3f", [128, 48], BF16)
        xb3s = S("xb3s", [3, D], BF16)
        wslab = S("wslab", [128, 16 * 512], BF16)
        rawx = S("rawx", [128, 1028], F32)
        cacc = S("cacc", [128, TC], F32)
        qkvt = S("qkvt", [128, TC], BF16)
        convw_sb = S("convw_sb", [128, 256], F32)
        convb_sb = S("convb_sb", [128, 64], F32)
        sq = S("sq", [128, TC], BF16)
        rrow = S("rrow", [1, 512], F32)
        rrowb = S("rrowb", [1, 512], BF16)
        ba_tm = S("ba_tm", [128, 8 * 64], F32)
        beta_tm = S("beta_tm", [128, 8 * VH], F32)
        g_tm = S("g_tm", [128, VH], F32)
        ghat_tm = S("ghat_tm", [128, 8 * VH], F32)
        egh = S("egh", [128, 8 * VH], F32)
        negegh = S("negegh", [128, 8 * VH], F32)
        ngh = S("ngh", [128, 8 * VH], F32)
        dtbbc = S("dtbbc", [128, VH], F32)
        nAbc = S("nAbc", [128, VH], F32)
        spt = S("spt", [128, VH], F32)
        spt2 = S("spt2", [128, VH], F32)
        ktile = S("ktile", [128, 128], BF16)
        qtile = S("qtile", [128, 128], BF16)
        vtile = S("vtile", [128, 128], BF16)
        kbn = S("kbn", [128, 128], BF16)
        kbp = S("kbp", [128, 128], BF16)
        ktm = S("ktm", [128, 128], BF16)
        vtm = S("vtm", [128, 128], BF16)
        kbart = S("kbart", [128, 128], BF16)
        brow = S("brow", [1, 128], F32)
        grow = S("grow", [1, 128], F32)
        argt = S("argt", [128, 128], F32)
        argt2 = S("argt2", [128, 128], F32)
        EA = S("EA", [128, 128], F32)
        EAt = S("EAt", [128, 128], F32)
        EM = S("EM", [128, 128], F32)
        M1t = S("M1t", [128, 128], BF16)
        Bf = S("Bf", [128, 128], F32)
        Btf = S("Btf", [128, 128], F32)
        Xa = S("Xa", [128, 128], F32)
        Xat = S("Xat", [128, 128], F32)
        Xb = S("Xb", [128, 128], F32)
        Xbt = S("Xbt", [128, 128], F32)
        BlowF = S("BlowF", [128, 128], F32)
        BlowTf = S("BlowTf", [128, 128], F32)
        TmF = S("TmF", [128, 128], F32)
        TmtF = S("TmtF", [128, 128], F32)
        Gm = S("Gm", [128, 128], F32)
        Gmt = S("Gmt", [128, 128], F32)
        Hm = S("Hm", [128, 128], F32)
        Hmt = S("Hmt", [128, 128], F32)
        H2m = S("H2m", [128, 128], F32)
        Tsb = S("Tsb", [128, 128], BF16)
        bd32 = S("bd32", [128, 128], F32)
        bl32 = S("bl32", [128, 128], F32)
        bl32T = S("bl32T", [128, 128], F32)
        uf = S("uf", [128, 256], F32)
        ub = S("ub", [128, 256], BF16)
        o1 = S("o1", [128, 128], BF16)
        kbarc = S("kbarc", [128, 1], F32)
        eEndc = S("eEndc", [128, 1], F32)
        Sst = S("Sst", [128, VH * 128], F32)
        Ssb = S("Ssb", [128, VH * 128], BF16)
        Mr = S("Mr", [128, VH * 128], BF16)
        MrT = S("MrT", [128, VH * 128], BF16)
        gu = S("gu", [128, 4 * TC], BF16)
        hscr = S("hscr", [128, TC], BF16)
        onesrf = S("onesrf", [1, 128], F32)
        onesrb = S("onesrb", [1, 128], BF16)
        onescb = S("onescb", [128, 1], BF16)
        identb = S("identb", [128, 128], BF16)
        identf = S("identf", [128, 128], F32)
        trif = S("trif", [128, 128], F32)
        maskLO = S("maskLO", [128, 128], F32)
        maskST = S("maskST", [128, 128], F32)
        cpool = S("cpool", [128, 8], F32)

        pTb = stk.enter_context(nc.psum_tensor("pTb", [128, 128], BF16))
        pT1f = stk.enter_context(nc.psum_tensor("pT1f", [128, 128], F32))
        pGa = stk.enter_context(nc.psum_tensor("pGa", [128, 128], F32))
        pGb = stk.enter_context(nc.psum_tensor("pGb", [128, 128], F32))
        pU = stk.enter_context(nc.psum_tensor("pU", [128, 256], F32))
        pS2 = stk.enter_context(nc.psum_tensor("pS2", [128, 128], F32))
        prow = stk.enter_context(nc.psum_tensor("prow", [1, 512], F32))
        pbig = stk.enter_context(nc.psum_tensor("pbig", [128, 512], F32))

        sems = {n: stk.enter_context(nc.semaphore(n)) for n in Sched.SEMS}
        blk = stk.enter_context(nc.Block())

        for i, val in enumerate((EPS0, 1.0, 0.0)):
            sch.v(lambda e, i=i, val=val: e.memset(cpool[:, i:i + 1], float(val)))
            nc.const_aps.register(F32, float(val), cpool[:, i:i + 1])

        for k in range(KD):
            sch.dma(xnf[:, k * TC:(k + 1) * TC], i_xnf[k])
        sch.dma(xb3s[:], i_xb3)
        sch.dma(convw_sb[:], convw)
        sch.dma(convb_sb[:], convb)
        sch.dma(onesrf[:], CONST["ones_row"])
        sch.dma(onesrb[:], CONST["ones_row"], cast=True)
        sch.dma(onescb[:], CONST["ones_col"], cast=True)
        sch.dma(identb[:], CONST["ident"], cast=True)
        sch.dma(identf[:], CONST["ident"])
        sch.dma(trif[:], CONST["tri_incl"])
        sch.dma(maskLO[:], CONST["mask_lowstrict_add"])
        sch.dma(maskST[:], CONST["mask_strict_add"])
        sch.dma(bd32[:], CONST["bd32"])
        sch.dma(bl32[:], CONST["bl32"])
        sch.dma(bl32T[:], CONST["bl32T"])
        for k in range(KD):
            sch.t(lambda e, k=k: e.transpose(pTb[:, 0:3], xb3s[:, k * 128:(k + 1) * 128],
                                             identb[0:3, 0:3]))
            sch.v(lambda e, k=k: e.tensor_copy(out=xb3f[:, k * 3:(k + 1) * 3], in_=pTb[:, 0:3]))

        def load_slab(wdram_og):
            wr = wdram_og.rearrange("(k p) o -> k p o", p=128)
            for k in range(KD):
                sch.dma(wslab[:, k * 512:(k + 1) * 512], wr[k], cast=True)

        def mm_acc(ptile, lhs_off, tt0, nfree):
            for k in range(KD):
                sch.t(lambda e, k=k, lhs_off=lhs_off, tt0=tt0, nfree=nfree:
                      e.matmul(ptile[:, 0:nfree],
                               wslab[:, k * 512 + lhs_off:k * 512 + lhs_off + 128],
                               xnf[:, k * TC + tt0:k * TC + tt0 + nfree],
                               start=(k == 0), stop=(k == KD - 1)))

        for og in range(16):
            load_slab(Wqkv[og])
            for oc in range(4):
                ocg = og * 4 + oc
                for k in range(KD):
                    sch.t(lambda e, k=k, oc=oc: e.matmul(
                        pS2[:, 0:3], wslab[:, k * 512 + oc * 128:k * 512 + oc * 128 + 128],
                        xb3f[:, k * 3:(k + 1) * 3], start=(k == 0), stop=(k == KD - 1)))
                sch.v(lambda e: e.tensor_copy(out=rawx[:, 0:3], in_=pS2[:, 0:3]))
                for tt in range(2):
                    mm_acc(pbig, oc * 128, tt * 512, 512)
                    sch.v(lambda e, tt=tt: e.tensor_copy(
                        out=rawx[:, 3 + tt * 512:3 + (tt + 1) * 512], in_=pbig[:, 0:512]))
                sch.v(lambda e, ocg=ocg: e.tensor_scalar(
                    out=cacc[:], in0=rawx[:, 3:1027],
                    scalar1=convw_sb[:, ocg:ocg + 1], scalar2=None,
                    op0=OP.mult, op1=OP.bypass))
                for i in range(1, 4):
                    sch.v(lambda e, i=i, ocg=ocg: e.scalar_tensor_tensor(
                        out=cacc[:], in0=rawx[:, 3 - i:3 - i + 1024],
                        scalar=convw_sb[:, i * 64 + ocg:i * 64 + ocg + 1],
                        in1=cacc[:], op0=OP.mult, op1=OP.add))
                sch.s(lambda e, ocg=ocg: e.activation(out=qkvt[:], in_=cacc[:], func=AF.Silu,
                                                      bias=convb_sb[:, ocg:ocg + 1], scale=1.0))
                sch.dma(qkv_scr[ocg], qkvt[:])

        for og in range(8):
            load_slab(Wz1[og])
            for oc in range(4):
                for tt in range(2):
                    mm_acc(pbig, oc * 128, tt * 512, 512)
                    sch.s(lambda e, tt=tt: e.activation(out=qkvt[:, tt * 512:(tt + 1) * 512],
                                                        in_=pbig[:, 0:512], func=AF.Sigmoid))
                sch.dma(o_zsig[og * 4 + oc], qkvt[:])

        wr = Wba.rearrange("(k p) o -> k p o", p=128)
        for k in range(KD):
            sch.dma(wslab[:, k * 512:k * 512 + 64], wr[k], cast=True)
        sch.dma(rrow[0:1, 0:VH], dtb1)
        sch.t(lambda e: e.matmul(pT1f[:, 0:VH], onesrf[:], rrow[0:1, 0:VH],
                                 start=True, stop=True))
        sch.v(lambda e: e.tensor_copy(out=dtbbc[:], in_=pT1f[:, 0:VH]))
        sch.dma(rrow[0:1, 0:VH], Alog1)
        sch.s(lambda e: e.activation(out=rrow[0:1, 0:VH], in_=rrow[0:1, 0:VH], func=AF.Exp))
        sch.v(lambda e: e.tensor_scalar(out=rrow[0:1, 0:VH], in0=rrow[0:1, 0:VH],
                                        scalar1=-1.0, scalar2=None, op0=OP.mult, op1=OP.bypass))
        sch.t(lambda e: e.matmul(pT1f[:, 0:VH], onesrf[:], rrow[0:1, 0:VH],
                                 start=True, stop=True))
        sch.v(lambda e: e.tensor_copy(out=nAbc[:], in_=pT1f[:, 0:VH]))
        for tc_ in range(8):
            for k in range(KD):
                sch.t(lambda e, k=k, tc_=tc_: e.matmul(
                    pT1f[:, 0:64], xnf[:, k * TC + tc_ * 128:k * TC + (tc_ + 1) * 128],
                    wslab[:, k * 512:k * 512 + 64], start=(k == 0), stop=(k == KD - 1)))
            sch.v(lambda e, tc_=tc_: e.tensor_copy(out=ba_tm[:, tc_ * 64:(tc_ + 1) * 64],
                                                   in_=pT1f[:, 0:64]))
            sch.s(lambda e, tc_=tc_: e.activation(out=beta_tm[:, tc_ * VH:(tc_ + 1) * VH],
                                                  in_=ba_tm[:, tc_ * 64:tc_ * 64 + VH],
                                                  func=AF.Sigmoid))
            sch.v(lambda e, tc_=tc_: e.tensor_tensor(out=g_tm[:],
                                                     in0=ba_tm[:, tc_ * 64 + VH:(tc_ + 1) * 64],
                                                     in1=dtbbc[:], op=OP.add))
            sch.s(lambda e: e.activation(out=spt[:], in_=g_tm[:], func=AF.Abs))
            sch.s(lambda e: e.activation(out=spt[:], in_=spt[:], func=AF.Exp, bias=0.0,
                                         scale=-1.0))
            sch.s(lambda e: e.activation(out=spt[:], in_=spt[:], func=AF.Ln, bias=1.0,
                                         scale=1.0))
            sch.v(lambda e: e.tensor_scalar(out=spt2[:], in0=g_tm[:], scalar1=0.0,
                                            scalar2=None, op0=OP.max, op1=OP.bypass))
            sch.v(lambda e: e.tensor_tensor(out=spt[:], in0=spt[:], in1=spt2[:], op=OP.add))
            sch.v(lambda e: e.tensor_tensor(out=g_tm[:], in0=spt[:], in1=nAbc[:], op=OP.mult))
            sch.t(lambda e: e.matmul(pT1f[:, 0:VH], trif[:], g_tm[:], start=True, stop=True))
            sch.v(lambda e, tc_=tc_: e.tensor_copy(out=ghat_tm[:, tc_ * VH:(tc_ + 1) * VH],
                                                   in_=pT1f[:, 0:VH]))
            sch.s(lambda e, tc_=tc_: e.activation(out=egh[:, tc_ * VH:(tc_ + 1) * VH],
                                                  in_=ghat_tm[:, tc_ * VH:(tc_ + 1) * VH],
                                                  func=AF.Exp))
            sch.v(lambda e, tc_=tc_: e.tensor_scalar(out=negegh[:, tc_ * VH:(tc_ + 1) * VH],
                                                     in0=egh[:, tc_ * VH:(tc_ + 1) * VH],
                                                     scalar1=-1.0, scalar2=None,
                                                     op0=OP.mult, op1=OP.bypass))
            sch.v(lambda e, tc_=tc_: e.tensor_scalar(out=ngh[:, tc_ * VH:(tc_ + 1) * VH],
                                                     in0=ghat_tm[:, tc_ * VH:(tc_ + 1) * VH],
                                                     scalar1=-1.0, scalar2=None,
                                                     op0=OP.mult, op1=OP.bypass))
            sch.dma(o_ghat[tc_ * 128:(tc_ + 1) * 128, :], ghat_tm[:, tc_ * VH:(tc_ + 1) * VH])

        for kh in range(16):
            for (src, isq) in ((kh, True), (16 + kh, False)):
                sch.dma(qkvt[:], qkv_scr[src])
                sch.s(lambda e: e.activation(out=sq[:], in_=qkvt[:], func=AF.Square))
                for tt in range(2):
                    sch.t(lambda e, tt=tt: e.matmul(prow[:, 0:512], onescb[:],
                                                    sq[:, tt * 512:(tt + 1) * 512],
                                                    start=True, stop=True))
                    sch.s(lambda e: e.activation(out=rrow[:], in_=prow[:, 0:512],
                                                 func=AF.Sqrt, bias=EPS0, scale=1.0))
                    sch.v(lambda e: e.reciprocal(out=rrow[:], in_=rrow[:]))
                    if isq:
                        sch.v(lambda e: e.tensor_scalar(out=rrow[:], in0=rrow[:],
                                                        scalar1=SQ_DK, scalar2=None,
                                                        op0=OP.mult, op1=OP.bypass))
                    sch.v(lambda e: e.tensor_copy(out=rrowb[:], in_=rrow[:]))
                    sch.t(lambda e: e.matmul(pbig[:, 0:512], onesrb[:], rrowb[:],
                                             start=True, stop=True))
                    sch.v(lambda e, tt=tt: e.tensor_tensor(
                        out=qkvt[:, tt * 512:(tt + 1) * 512],
                        in0=qkvt[:, tt * 512:(tt + 1) * 512],
                        in1=pbig[:, 0:512], op=OP.mult))
                sch.dma(qkv_scr[src], qkvt[:])
                if isq:
                    sch.dma(o_qhat[kh], qkvt[:])

        for vh in range(VH):
            sch.v(lambda e, vh=vh: e.memset(Sst[:, vh * 128:(vh + 1) * 128], 0.0))
            sch.v(lambda e, vh=vh: e.memset(Ssb[:, vh * 128:(vh + 1) * 128], 0.0))
            sch.v(lambda e, vh=vh: e.tensor_copy(out=Mr[:, vh * 128:(vh + 1) * 128],
                                                 in_=identb[:]))
            sch.v(lambda e, vh=vh: e.tensor_copy(out=MrT[:, vh * 128:(vh + 1) * 128],
                                                 in_=identb[:]))
        for c in range(8):
            t0 = c * 128
            for vh in range(VH):
                kh = vh // 2
                sch.dma(ktile[:], qkv_scr[16 + kh][:, t0:t0 + 128])
                sch.dma(qtile[:], qkv_scr[kh][:, t0:t0 + 128])
                sch.dma(vtile[:], qkv_scr[32 + vh][:, t0:t0 + 128])
                sch.t(lambda e, c=c, vh=vh: e.transpose(
                    pT1f[0:1, 0:128], beta_tm[:, c * VH + vh:c * VH + vh + 1], identf[:]))
                sch.v(lambda e: e.tensor_copy(out=brow[:], in_=pT1f[0:1, 0:128]))
                sch.v(lambda e: e.tensor_scalar(out=rrowb[0:1, 0:128], in0=brow[:],
                                                scalar1=-1.0, scalar2=None,
                                                op0=OP.mult, op1=OP.bypass))
                sch.t(lambda e: e.matmul(pGa[:], onesrb[:], rrowb[0:1, 0:128],
                                         start=True, stop=True))
                sch.v(lambda e: e.tensor_tensor(out=kbn[:], in0=ktile[:], in1=pGa[:],
                                                op=OP.mult))
                sch.v(lambda e: e.tensor_scalar(out=kbp[:], in0=kbn[:], scalar1=-1.0,
                                                scalar2=None, op0=OP.mult, op1=OP.bypass))
                sch.t(lambda e, c=c, vh=vh: e.transpose(
                    pT1f[0:1, 0:128], ghat_tm[:, c * VH + vh:c * VH + vh + 1], identf[:]))
                sch.v(lambda e: e.tensor_copy(out=grow[:], in_=pT1f[0:1, 0:128]))
                sch.t(lambda e: e.matmul(pT1f[:], onesrf[:], grow[:], start=True, stop=True))
                sch.v(lambda e: e.scalar_tensor_tensor(out=argt[:], in0=pT1f[:], scalar=-1.0,
                                                       in1=maskLO[:], op0=OP.mult, op1=OP.add))
                sch.s(lambda e, c=c, vh=vh: e.activation(out=EA[:], in_=argt[:], func=AF.Exp,
                                                         bias=ghat_tm[:, c * VH + vh:c * VH + vh + 1],
                                                         scale=1.0))
                sch.v(lambda e: e.scalar_tensor_tensor(out=argt2[:], in0=pT1f[:], scalar=1.0,
                                                       in1=maskST[:], op0=OP.mult, op1=OP.add))
                sch.s(lambda e, c=c, vh=vh: e.activation(out=EAt[:], in_=argt2[:], func=AF.Exp,
                                                         bias=ngh[:, c * VH + vh:c * VH + vh + 1],
                                                         scale=1.0))
                sch.v(lambda e: e.tensor_tensor(out=EM[:], in0=EAt[:], in1=identf[:],
                                                op=OP.add))
                sch.t(lambda e: e.matmul(pGa[:], ktile[:], kbn[:], start=True, stop=True))
                sch.v(lambda e: e.tensor_tensor(out=Bf[:], in0=EA[:], in1=pGa[:], op=OP.mult))
                sch.t(lambda e: e.matmul(pGb[:], kbn[:], ktile[:], start=True, stop=True))
                sch.v(lambda e: e.tensor_tensor(out=Btf[:], in0=EAt[:], in1=pGb[:], op=OP.mult))
                sch.t(lambda e: e.matmul(pGb[:], kbp[:], qtile[:], start=True, stop=True))
                sch.v(lambda e: e.tensor_tensor(out=M1t[:], in0=EM[:], in1=pGb[:], op=OP.mult))
                sch.t(lambda e: e.transpose(pTb[:], vtile[:], identb[:]))
                sch.v(lambda e: e.tensor_copy(out=vtm[:], in_=pTb[:]))
                sch.t(lambda e: e.transpose(pTb[:], ktile[:], identb[:]))
                sch.v(lambda e: e.tensor_copy(out=ktm[:], in_=pTb[:]))
                sch.t(lambda e, vh=vh: e.matmul(pU[:, 0:128], ktile[:],
                                                Ssb[:, vh * 128:(vh + 1) * 128],
                                                start=True, stop=True))
                sch.v(lambda e, c=c, vh=vh: e.scalar_tensor_tensor(
                    out=uf[:, 0:128], in0=pU[:, 0:128],
                    scalar=negegh[:, c * VH + vh:c * VH + vh + 1], in1=vtm[:],
                    op0=OP.mult, op1=OP.add))
                sch.v(lambda e, c=c, vh=vh: e.tensor_scalar(out=uf[:, 128:256], in0=ktm[:],
                                                            scalar1=egh[:, c * VH + vh:c * VH + vh + 1],
                                                            scalar2=None,
                                                            op0=OP.mult, op1=OP.bypass))
                sch.v(lambda e: e.tensor_copy(out=ub[:], in_=uf[:]))
                sch.v(lambda e: e.tensor_tensor(out=Xa[:], in0=Bf[:], in1=bd32[:], op=OP.mult))
                sch.v(lambda e: e.tensor_tensor(out=Xat[:], in0=Btf[:], in1=bd32[:], op=OP.mult))
                sch.v(lambda e: e.tensor_tensor(out=BlowF[:], in0=Bf[:], in1=bl32[:], op=OP.mult))
                sch.v(lambda e: e.tensor_tensor(out=BlowTf[:], in0=Btf[:], in1=bl32T[:], op=OP.mult))
                sch.v(lambda e: e.tensor_tensor(out=TmF[:], in0=Xa[:], in1=identf[:], op=OP.add))
                sch.v(lambda e: e.tensor_tensor(out=TmtF[:], in0=Xat[:], in1=identf[:], op=OP.add))
                cur = [Xa, Xat, Xb, Xbt]
                for l in range(1, 5):
                    Xc, Xct, Xn, Xnt = cur
                    sch.t(lambda e, Xc=Xc, Xct=Xct: e.matmul(pGa[:], Xct[:], Xc[:],
                                                             start=True, stop=True))
                    sch.v(lambda e, Xn=Xn: e.tensor_copy(out=Xn[:], in_=pGa[:]))
                    sch.t(lambda e, Xc=Xc, Xct=Xct: e.matmul(pGb[:], Xc[:], Xct[:],
                                                             start=True, stop=True))
                    sch.v(lambda e, Xnt=Xnt: e.tensor_copy(out=Xnt[:], in_=pGb[:]))
                    sch.t(lambda e, Xnt=Xnt: e.matmul(pGa[:], Xnt[:], TmF[:],
                                                      start=True, stop=True))
                    sch.t(lambda e, Xnt=Xnt: e.matmul(pGb[:], TmF[:], Xnt[:],
                                                      start=True, stop=True))
                    sch.v(lambda e: e.tensor_tensor(out=TmF[:], in0=TmF[:], in1=pGa[:],
                                                    op=OP.add))
                    sch.v(lambda e: e.tensor_tensor(out=TmtF[:], in0=TmtF[:], in1=pGb[:],
                                                    op=OP.add))
                    cur = [Xn, Xnt, Xc, Xct]
                sch.t(lambda e: e.matmul(pGa[:], TmtF[:], BlowF[:], start=True, stop=True))
                sch.v(lambda e: e.tensor_copy(out=Gm[:], in_=pGa[:]))
                sch.t(lambda e: e.matmul(pGb[:], BlowF[:], TmtF[:], start=True, stop=True))
                sch.v(lambda e: e.tensor_copy(out=Gmt[:], in_=pGb[:]))
                sch.v(lambda e: e.tensor_tensor(out=Hm[:], in0=Gm[:], in1=identf[:], op=OP.add))
                sch.t(lambda e: e.matmul(pGa[:], Gmt[:], Hm[:], start=True, stop=True))
                sch.v(lambda e: e.tensor_tensor(out=H2m[:], in0=pGa[:], in1=identf[:], op=OP.add))
                sch.t(lambda e: e.matmul(pGb[:], H2m[:], Gmt[:], start=True, stop=True))
                sch.v(lambda e: e.tensor_tensor(out=Hmt[:], in0=pGb[:], in1=identf[:], op=OP.add))
                sch.t(lambda e: e.matmul(pGa[:], TmF[:], Hmt[:], start=True, stop=True))
                sch.v(lambda e: e.tensor_copy(out=Tsb[:], in_=pGa[:]))
                sch.t(lambda e: e.matmul(pU[:], Tsb[:], ub[:], start=True, stop=True))
                sch.v(lambda e: e.tensor_copy(out=uf[:], in_=pU[:]))
                sch.v(lambda e: e.tensor_copy(out=ub[:], in_=uf[:]))
                sch.t(lambda e, vh=vh: e.matmul(pU[:, 0:128], qtile[:],
                                                Ssb[:, vh * 128:(vh + 1) * 128],
                                                start=True, stop=True))
                sch.v(lambda e, c=c, vh=vh: e.tensor_scalar(out=o1[:], in0=pU[:, 0:128],
                                                            scalar1=egh[:, c * VH + vh:c * VH + vh + 1],
                                                            scalar2=None,
                                                            op0=OP.mult, op1=OP.bypass))
                sch.t(lambda e: e.matmul(pGb[:], M1t[:], ub[:, 0:128], start=True, stop=True))
                sch.v(lambda e: e.tensor_tensor(out=o1[:], in0=o1[:], in1=pGb[:], op=OP.add))
                sch.dma(o_Opre[t0:t0 + 128, vh * 128:(vh + 1) * 128], o1[:])
                sch.v(lambda e, c=c, vh=vh: e.tensor_scalar(out=kbarc[:], in0=EM[:, 127:128],
                                                            scalar1=beta_tm[:, c * VH + vh:c * VH + vh + 1],
                                                            scalar2=None,
                                                            op0=OP.mult, op1=OP.bypass))
                sch.v(lambda e: e.tensor_scalar(out=kbart[:], in0=ktm[:], scalar1=kbarc[:],
                                                scalar2=None, op0=OP.mult, op1=OP.bypass))
                sch.t(lambda e: e.matmul(pT1f[:, 0:1], onesrf[:], grow[0:1, 127:128],
                                         start=True, stop=True))
                sch.s(lambda e: e.activation(out=eEndc[:], in_=pT1f[:, 0:1], func=AF.Exp))
                sch.t(lambda e: e.matmul(pS2[:], kbart[:], ub[:, 0:128], start=True, stop=True))
                sch.v(lambda e, vh=vh: e.scalar_tensor_tensor(
                    out=Sst[:, vh * 128:(vh + 1) * 128], in0=Sst[:, vh * 128:(vh + 1) * 128],
                    scalar=eEndc[:], in1=pS2[:], op0=OP.mult, op1=OP.add))
                sch.v(lambda e, vh=vh: e.tensor_copy(out=Ssb[:, vh * 128:(vh + 1) * 128],
                                                     in_=Sst[:, vh * 128:(vh + 1) * 128]))
                sch.t(lambda e: e.matmul(pGa[:], ub[:, 128:256], kbart[:],
                                         start=True, stop=True))
                sch.v(lambda e: e.scalar_tensor_tensor(out=M1t[:], in0=identb[:],
                                                       scalar=eEndc[:], in1=pGa[:],
                                                       op0=OP.mult, op1=OP.subtract))
                sch.dma(o_McT[c][vh], M1t[:])
                sch.t(lambda e: e.matmul(pGb[:], kbart[:], ub[:, 128:256],
                                         start=True, stop=True))
                sch.v(lambda e: e.scalar_tensor_tensor(out=EA[:], in0=identb[:],
                                                       scalar=eEndc[:], in1=pGb[:],
                                                       op0=OP.mult, op1=OP.subtract))
                sch.t(lambda e, vh=vh: e.matmul(pGa[:], Mr[:, vh * 128:(vh + 1) * 128],
                                                M1t[:], start=True, stop=True))
                sch.t(lambda e, vh=vh: e.matmul(pGb[:], M1t[:],
                                                Mr[:, vh * 128:(vh + 1) * 128],
                                                start=True, stop=True))
                sch.v(lambda e, vh=vh: e.tensor_copy(out=MrT[:, vh * 128:(vh + 1) * 128],
                                                     in_=pGa[:]))
                sch.v(lambda e, vh=vh: e.tensor_copy(out=Mr[:, vh * 128:(vh + 1) * 128],
                                                     in_=pGb[:]))
        for vh in range(VH):
            sch.dma(o_MrT[vh], MrT[:, vh * 128:(vh + 1) * 128])
            sch.dma(o_Sloc[vh], Ssb[:, vh * 128:(vh + 1) * 128])

        scl_g = S("scl_g", [128, 44], F32)
        scl_u = S("scl_u", [128, 44], F32)
        scl_d = S("scl_d", [128, 16], F32)
        sch.dma(scl_g[:], gs1)
        sch.dma(scl_u[:], us1)
        sch.dma(scl_d[:], ds1)
        for og in range(11):
            load_slab(Wg1[og])
            for oc in range(4):
                kg = og * 4 + oc
                for tt in range(2):
                    mm_acc(pbig, oc * 128, tt * 512, 512)
                    sch.s(lambda e, oc=oc, tt=tt, kg=kg: e.activation(
                        out=gu[:, oc * TC + tt * 512:oc * TC + (tt + 1) * 512],
                        in_=pbig[:, 0:512],
                        func=AF.Silu, bias=0.0, scale=scl_g[:, kg:kg + 1]))
            load_slab(Wu1[og])
            for oc in range(4):
                kg = og * 4 + oc
                for tt in range(2):
                    mm_acc(pbig, oc * 128, tt * 512, 512)
                    sch.v(lambda e, tt=tt, kg=kg: e.tensor_scalar(
                        out=hscr[:, tt * 512:(tt + 1) * 512], in0=pbig[:, 0:512],
                        scalar1=scl_u[:, kg:kg + 1], scalar2=None,
                        op0=OP.mult, op1=OP.bypass))
                sch.v(lambda e, oc=oc: e.tensor_tensor(
                    out=hscr[:], in0=hscr[:], in1=gu[:, oc * TC:(oc + 1) * TC], op=OP.mult))
                sch.dma(hdn_scr[kg], hscr[:])

        wdt = S("wdt", [128, 128], BF16)
        hdt = S("hdt", [128, 512], BF16)
        ffnf = S("ffnf", [128, 512], F32)
        for tt in range(2):
            for ocd in range(16):
                for k in range(44):
                    sch.dma(wdt[:], Wd1[ocd // 4][k * 128:(k + 1) * 128,
                                                  (ocd % 4) * 128:(ocd % 4 + 1) * 128],
                            cast=True)
                    sch.dma(hdt[:], hdn_scr[k][:, tt * 512:(tt + 1) * 512])
                    sch.t(lambda e, k=k: e.matmul(pbig[:, 0:512], wdt[:], hdt[:],
                                                  start=(k == 0), stop=(k == 43)))
                sch.v(lambda e, ocd=ocd: e.tensor_scalar(out=ffnf[:], in0=pbig[:, 0:512],
                                                         scalar1=scl_d[:, ocd:ocd + 1],
                                                         scalar2=None, op0=OP.mult,
                                                         op1=OP.bypass))
                for b in range(4):
                    sch.t(lambda e, b=b: e.transpose(pT1f[:], ffnf[:, b * 128:(b + 1) * 128],
                                                     identf[:]))
                    sch.v(lambda e: e.tensor_copy(out=argt[:], in_=pT1f[:]))
                    sch.dma(o_ffn1[tt * 512 + b * 128:tt * 512 + (b + 1) * 128,
                                   ocd * 128:(ocd + 1) * 128], argt[:])
        sch.build(blk, sems)
    return nc




def prep_D_weights(W):
    f = np.float32
    cw = np.asarray(W["L1_attn_conv_w"], f)
    convw_fm = np.ascontiguousarray(cw.reshape(64, 128, 4).transpose(1, 2, 0).reshape(128, 256))
    convb_fm = np.ascontiguousarray(np.asarray(W["L1_attn_conv_b"], f).reshape(64, 128).T)
    out = {
        "Wqkv": prep_og(np.ascontiguousarray(np.asarray(W["L1_attn_W_qkv"], f))),
        "Wz1": prep_og(np.ascontiguousarray(np.asarray(W["L1_attn_W_z"], f))),
        "Wba": np.ascontiguousarray(np.concatenate(
            [np.asarray(W["L1_attn_W_b"], f), np.asarray(W["L1_attn_W_a"], f)], axis=1)),
        "convw": convw_fm.astype(f),
        "convb": convb_fm.astype(f),
        "dtb1": np.asarray(W["L1_attn_dt_bias"], f).reshape(1, VH),
        "Alog1": np.asarray(W["L1_attn_A_log"], f).reshape(1, VH),
        "Wg1": prep_og(np.ascontiguousarray(np.asarray(W["L1_ffn_gate_w"], f).T)),
        "Wu1": prep_og(np.ascontiguousarray(np.asarray(W["L1_ffn_up_w"], f).T)),
        "Wd1": prep_og(np.ascontiguousarray(np.asarray(W["L1_ffn_down_w"], f).T)),
        "gs1": np.ascontiguousarray(np.asarray(W["L1_ffn_gate_s"], f).reshape(44, 128).T),
        "us1": np.ascontiguousarray(np.asarray(W["L1_ffn_up_s"], f).reshape(44, 128).T),
        "ds1": np.ascontiguousarray(np.asarray(W["L1_ffn_down_s"], f).reshape(16, 128).T),
    }
    for k, v in host_consts().items():
        out["c_" + k] = v
    return out


def run_D(inputs, c_results):
    nc = build_auto(build_D)
    wmap = prep_D_weights(inputs)
    in_maps = []
    for r in range(NCORES):
        m = dict(wmap)
        m["i_xnf"] = c_results[r]["o_xn1f"]
        if r == 0:
            m["i_xb3"] = np.zeros((3, D), np.float32).astype(
                c_results[0]["o_xn1tm"].dtype)
        else:
            m["i_xb3"] = c_results[r - 1]["o_xn1tm"][TC - 3:TC, :]
        in_maps.append(m)
    res = run_bass_kernel_spmd(nc, in_maps, core_ids=list(range(NCORES)))
    return res.results




def build_E():
    nc = bass.Bass("TRN2", target_bir_lowering=False, name="mkE")

    def DR(n, sh, dt, k=None):
        return nc.dram_tensor(n, list(sh), dt, kind=k).ap() if k else nc.dram_tensor(n, list(sh), dt).ap()

    i_Opre = DR("i_Opre", (TC, VDIM), BF16, "ExternalInput")
    i_qhat = DR("i_qhat", (16, 128, TC), BF16, "ExternalInput")
    i_ghat = DR("i_ghat", (TC, VH), F32, "ExternalInput")
    i_McT = DR("i_McT", (8, VH, 128, 128), BF16, "ExternalInput")
    i_MrTs = DR("i_MrTs", (7, VH, 128, 128), BF16, "ExternalInput")
    i_Slocs = DR("i_Slocs", (7, VH, 128, 128), BF16, "ExternalInput")
    i_zsig = DR("i_zsig", (32, 128, TC), BF16, "ExternalInput")
    i_x1 = DR("i_x1", (TC, D), F32, "ExternalInput")
    i_ffn1 = DR("i_ffn1", (TC, D), F32, "ExternalInput")
    Wo1 = DR("Wo1", (4, VDIM, 512), F32, "ExternalInput")
    ngam = DR("ngam", (DV, 1), F32, "ExternalInput")
    HC = host_consts()
    CONST = {k: DR("c_" + k, v.shape, F32, "ExternalInput") for k, v in HC.items()}

    o_out = DR("o_out", (TC, D), F32, "ExternalOutput")
    gated_scr = DR("gated_scr", (32, 128, TC), BF16)

    sch = Sched()
    with ExitStack() as stk:
        def S(n, sh, dt):
            return stk.enter_context(nc.sbuf_tensor(n, list(sh), dt))

        Tst = S("Tst", [128, VH * 128], F32)
        Tsb = S("Tsb", [128, VH * 128], BF16)
        mstg = S("mstg", [128, 128], BF16)
        sstg = S("sstg", [128, 128], BF16)
        qstg = S("qstg", [128, 128], BF16)
        ostg = S("ostg", [128, 128], BF16)
        zstg = S("zstg", [128, 128], BF16)
        gstg = S("gstg", [128, 128], BF16)
        sqt = S("sqt", [128, 128], BF16)
        ghat_tm = S("ghat_tm", [128, 8 * VH], F32)
        egh = S("egh", [128, 8 * VH], F32)
        sscol = S("sscol", [128, 1], F32)
        rstdc = S("rstdc", [128, 1], F32)
        ngамc = S("ngamc", [128, 1], F32)
        wslab = S("wslab", [128, 32 * 512], BF16)
        x1s = S("x1s", [128, 512], F32)
        ffs = S("ffs", [128, 512], F32)
        outs = S("outs", [128, 512], F32)
        onesrf = S("onesrf", [1, 128], F32)
        identb = S("identb", [128, 128], BF16)
        cpool = S("cpool", [128, 8], F32)

        pC = stk.enter_context(nc.psum_tensor("pC", [128, 128], F32))
        pC2 = stk.enter_context(nc.psum_tensor("pC2", [128, 128], F32))
        pTb = stk.enter_context(nc.psum_tensor("pTb", [128, 128], BF16))
        pW = stk.enter_context(nc.psum_tensor("pW", [128, 512], F32))

        sems = {n: stk.enter_context(nc.semaphore(n)) for n in Sched.SEMS}
        blk = stk.enter_context(nc.Block())

        for i, val in enumerate((EPS0, 1.0, 0.0)):
            sch.v(lambda e, i=i, val=val: e.memset(cpool[:, i:i + 1], float(val)))
            nc.const_aps.register(F32, float(val), cpool[:, i:i + 1])

        sch.dma(onesrf[:], CONST["ones_row"])
        sch.dma(identb[:], CONST["ident"], cast=True)
        sch.dma(ngамc[:], ngam)
        gr = i_ghat.rearrange("(c p) h -> c p h", p=128)
        for c in range(8):
            sch.dma(ghat_tm[:, c * VH:(c + 1) * VH], gr[c])
            sch.s(lambda e, c=c: e.activation(out=egh[:, c * VH:(c + 1) * VH],
                                              in_=ghat_tm[:, c * VH:(c + 1) * VH],
                                              func=AF.Exp))

        sch.v(lambda e: e.memset(Tst[:], 0.0))
        sch.v(lambda e: e.memset(Tsb[:], 0.0))
        for q in range(7):
            for vh in range(VH):
                sch.dma(mstg[:], i_MrTs[q][vh])
                sch.dma(sstg[:], i_Slocs[q][vh])
                sch.t(lambda e, vh=vh: e.matmul(pC[:], mstg[:],
                                                Tsb[:, vh * 128:(vh + 1) * 128],
                                                start=True, stop=True))
                sch.v(lambda e, vh=vh: e.tensor_tensor(out=Tst[:, vh * 128:(vh + 1) * 128],
                                                       in0=pC[:], in1=sstg[:], op=OP.add))
                sch.v(lambda e, vh=vh: e.tensor_copy(out=Tsb[:, vh * 128:(vh + 1) * 128],
                                                     in_=Tst[:, vh * 128:(vh + 1) * 128]))

        for c in range(8):
            t0 = c * 128
            for vh in range(VH):
                kh = vh // 2
                sch.dma(qstg[:], i_qhat[kh][:, t0:t0 + 128])
                sch.dma(ostg[:], i_Opre[t0:t0 + 128, vh * 128:(vh + 1) * 128])
                sch.dma(zstg[:], i_zsig[vh][:, t0:t0 + 128])
                sch.t(lambda e, vh=vh: e.matmul(pC[:], qstg[:],
                                                Tsb[:, vh * 128:(vh + 1) * 128],
                                                start=True, stop=True))
                sch.v(lambda e, c=c, vh=vh: e.scalar_tensor_tensor(
                    out=ostg[:], in0=pC[:], scalar=egh[:, c * VH + vh:c * VH + vh + 1],
                    in1=ostg[:], op0=OP.mult, op1=OP.add))
                sch.s(lambda e: e.activation(out=sqt[:], in_=ostg[:], func=AF.Square,
                                             accum_out=sscol[:]))
                sch.v(lambda e: e.tensor_scalar(out=sscol[:], in0=sscol[:],
                                                scalar1=1.0 / DV, scalar2=None,
                                                op0=OP.mult, op1=OP.bypass))
                sch.s(lambda e: e.activation(out=sscol[:], in_=sscol[:], func=AF.Sqrt,
                                             bias=EPS0, scale=1.0))
                sch.v(lambda e: e.reciprocal(out=rstdc[:], in_=sscol[:]))
                sch.v(lambda e: e.tensor_scalar(out=ostg[:], in0=ostg[:], scalar1=rstdc[:],
                                                scalar2=None, op0=OP.mult, op1=OP.bypass))
                sch.t(lambda e: e.transpose(pTb[:], ostg[:], identb[:]))
                sch.s(lambda e: e.activation(out=gstg[:], in_=pTb[:], func=AF.Copy,
                                             bias=0.0, scale=ngамc[:]))
                sch.v(lambda e: e.tensor_tensor(out=gstg[:], in0=gstg[:], in1=zstg[:],
                                                op=OP.mult))
                sch.dma(gated_scr[vh][:, t0:t0 + 128], gstg[:])
                sch.dma(mstg[:], i_McT[c][vh])
                sch.t(lambda e, vh=vh: e.matmul(pC2[:], mstg[:],
                                                Tsb[:, vh * 128:(vh + 1) * 128],
                                                start=True, stop=True))
                sch.v(lambda e, vh=vh: e.tensor_copy(out=Tst[:, vh * 128:(vh + 1) * 128],
                                                     in_=pC2[:]))
                sch.v(lambda e, vh=vh: e.tensor_copy(out=Tsb[:, vh * 128:(vh + 1) * 128],
                                                     in_=Tst[:, vh * 128:(vh + 1) * 128]))

        for og in range(4):
            wr = Wo1[og].rearrange("(k p) o -> k p o", p=128)
            for k in range(32):
                sch.dma(wslab[:, k * 512:(k + 1) * 512], wr[k], cast=True)
            for tc_ in range(8):
                for k in range(32):
                    sch.dma(gstg[:], gated_scr[k][:, tc_ * 128:(tc_ + 1) * 128])
                    sch.t(lambda e, k=k: e.matmul(pW[:, 0:512], gstg[:],
                                                  wslab[:, k * 512:(k + 1) * 512],
                                                  start=(k == 0), stop=(k == 31)))
                sch.dma(x1s[:], i_x1[tc_ * 128:(tc_ + 1) * 128, og * 512:(og + 1) * 512])
                sch.dma(ffs[:], i_ffn1[tc_ * 128:(tc_ + 1) * 128, og * 512:(og + 1) * 512])
                sch.v(lambda e: e.scalar_tensor_tensor(out=outs[:], in0=pW[:, 0:512],
                                                       scalar=1.0, in1=x1s[:],
                                                       op0=OP.mult, op1=OP.add))
                sch.v(lambda e: e.tensor_tensor(out=outs[:], in0=outs[:], in1=ffs[:],
                                                op=OP.add))
                sch.dma(o_out[tc_ * 128:(tc_ + 1) * 128, og * 512:(og + 1) * 512], outs[:])

        sch.build(blk, sems)
    return nc


def run_E(inputs, c_results, d_results):
    nc = build_auto(build_E)
    HC = host_consts()
    f = np.float32
    Wo1 = prep_og(np.ascontiguousarray(np.asarray(inputs["L1_attn_W_out"], f)))
    ngam = np.asarray(inputs["L1_attn_norm_gamma"], f).reshape(DV, 1)
    ident_b = np.eye(128, dtype=np.float32)
    in_maps = []
    for r in range(NCORES):
        d = d_results[r]
        MrTs = np.zeros((7, VH, 128, 128), f)
        Slocs = np.zeros((7, VH, 128, 128), f)
        for q in range(7):
            if q < r:
                MrTs[q] = d_results[q]["o_MrT"].astype(f)
                Slocs[q] = d_results[q]["o_Sloc"].astype(f)
            else:
                MrTs[q] = np.broadcast_to(ident_b, (VH, 128, 128))
        m = {"i_Opre": d["o_Opre"], "i_qhat": d["o_qhat"], "i_ghat": d["o_ghat"],
             "i_McT": d["o_McT"],
             "i_MrTs": MrTs.astype(np.asarray(d["o_MrT"]).dtype),
             "i_Slocs": Slocs.astype(np.asarray(d["o_Sloc"]).dtype),
             "i_zsig": d["o_zsig"], "i_x1": c_results[r]["o_x1"],
             "i_ffn1": d["o_ffn1"], "Wo1": Wo1, "ngam": ngam}
        for k, v in HC.items():
            m["c_" + k] = v
        in_maps.append(m)
    res = run_bass_kernel_spmd(nc, in_maps, core_ids=list(range(NCORES)))
    return res.results


def kernel(**inputs):
    resA = run_A(inputs)
    th_totals = [resA[r]["o_tht"] for r in range(NCORES)]
    resB = run_B(resA, th_totals)
    run_C.Woutt = prep_og(np.ascontiguousarray(
        np.asarray(inputs["L0_attn_Wout"], np.float32).T))
    resC = run_C(resA, resB)
    resD = run_D(inputs, resC)
    resE = run_E(inputs, resC, resD)
    out = np.concatenate([resE[r]["o_out"] for r in range(NCORES)], axis=0)
    return out.reshape(1, T, D).astype(np.float32)
